# version 40
# baseline (speedup 1.0000x reference)
import numpy as np
from contextlib import ExitStack

import concourse.bass as bass
import concourse.mybir as mybir
from concourse.bass_utils import run_bass_kernel_spmd

F32 = mybir.dt.float32
BF16 = mybir.dt.bfloat16
U8 = mybir.dt.uint8
AF = mybir.ActivationFunctionType
ALU = mybir.AluOpType

P = 128
T = 512
G = 8
FG = 32
D = 1024
FF = 4096
EPS = 1e-5
LN05 = float(np.log(0.5))
NWARM = 2
NMAIN = 8


class Ops:
    def __init__(self, nc):
        self.nc = nc
        self.lists = {k: [] for k in ("tensor", "vector", "scalar", "sync", "gpsimd")}
        self.cnt = {}
        self.hdl = {}
        self.waited = {k: {} for k in self.lists}

    def sem(self, stack, name):
        self.hdl[name] = stack.enter_context(self.nc.semaphore(name))
        self.cnt[name] = 0

    DMA_SEMS = ("s_x", "s_w0", "s_w1", "s_w2", "s_g", "s_g1", "s_g2", "s_y")

    phase = "init"

    def emit(self, eng, fn, waits=(), inc=None, inc_amt=1):
        if inc in self.DMA_SEMS and self.cnt[inc] > 0:
            waits = list(waits) + [(inc, self.cnt[inc])]
        w = []
        for s, v in waits:
            if v is None or v <= 0:
                continue
            if self.waited[eng].get(s, -1) >= v:
                continue
            self.waited[eng][s] = v
            w.append((self.hdl[s], v))
        entry = [w, fn, None, self.phase]
        if inc is not None:
            self.cnt[inc] += inc_amt
            entry[2] = (self.hdl[inc], self.cnt[inc], inc_amt)
        self.lists[eng].append(entry)
        return self.cnt[inc] if inc is not None else None

    def val(self, s):
        return self.cnt[s]


def build_nc(n_warm=NWARM, n_main=NMAIN, taps=()):
    nc = bass.Bass("TRN2", target_bir_lowering=False,
                   detect_race_conditions=False)
    NTILE = n_warm + n_main
    NTOK = NTILE * T
    taps = set(taps)

    xin_d = nc.dram_tensor("xin", [D, NTOK], F32, kind="ExternalInput").ap()
    gate_d = nc.dram_tensor("gate", [1, 1], F32, kind="ExternalInput").ap()
    vec_names = ["ln_att_g", "ln_att_b", "ln_ffn_g", "ln_ffn_b", "att_td", "att_tf",
                 "amix_k", "amix_v", "amix_r", "fmix_k", "fmix_r"]
    vecs_d = {n: nc.dram_tensor(n, [D], F32, kind="ExternalInput").ap() for n in vec_names}
    wsq_names = ["wk", "wv", "wr", "wo", "fwr"]
    wsq_d = {n: nc.dram_tensor(n, [D, D], F32, kind="ExternalInput").ap() for n in wsq_names}
    fwk_d = nc.dram_tensor("fwk", [D, FF], F32, kind="ExternalInput").ap()
    fwv_d = nc.dram_tensor("fwv", [FF, D], F32, kind="ExternalInput").ap()
    k0_d = nc.dram_tensor("k0", [2 * D, FF], U8, kind="ExternalInput").ap()
    k1_d = nc.dram_tensor("k1", [32, P, FF], U8, kind="ExternalInput").ap()
    k2_d = nc.dram_tensor("k2", [8, P, FF], U8, kind="ExternalInput").ap()
    s_d = {n: nc.dram_tensor(n, [1, 1], F32, kind="ExternalInput").ap()
           for n in ("s0", "s1", "s2")}
    y_d = nc.dram_tensor("y", [D, n_main * T], F32, kind="ExternalOutput").ap()

    wsq_s = {n: nc.dram_tensor(n + "S", [D, D], BF16, kind="Internal").ap()
             for n in wsq_names}
    fwk_s = nc.dram_tensor("fwkS", [D, FF], BF16, kind="Internal").ap()
    fwv_s = nc.dram_tensor("fwvS", [8, FF, P], BF16, kind="Internal").ap()
    w0_s = nc.dram_tensor("w0S", [D, FF], BF16, kind="Internal").ap()

    tap_d = {}
    for tn in taps:
        shp = {"h": G * (T + 1), "k": G * T, "v": G * T, "r": G * (T + 1),
               "wkv": G * T, "x1": G * T, "h2": G * (T + 1), "kf": FG * T,
               "x2": G * T, "x4": G * T, "h1s": FG * T, "h2s": FG * T,
               "pre": G * T, "rden": G * T, "scrP": G * (T + 1),
               "st8d": 9 * 8}[tn]
        tap_d[tn] = nc.dram_tensor("tap_" + tn, [P, shp], F32,
                                   kind="ExternalOutput").ap()

    with ExitStack() as stack:
        e = stack.enter_context
        sb = lambda name, cols, dt=F32: e(nc.sbuf_tensor(name, [P, cols], dt))
        wring = [sb(f"wring{i}", 4096, BF16) for i in range(3)]
        x0 = sb("x0", G * T)
        hbuf = sb("hbuf", G * (T + 1))
        scr = sb("scr", G * (T + 1))
        xbf = sb("xbf", G * T, BF16)
        xxa = sb("xxa", G * T, BF16)
        xxb = sb("xxb", G * T, BF16)
        kvs = sb("kvs", G * T)
        vvs = sb("vvs", G * T)
        big1 = sb("big1", FG * T, BF16)
        big2 = sb("big2", FG * T, BF16)
        stg = sb("stg", T)
        ewb = sb("ewb", T)
        ewb2 = sb("ewb2", T)
        dp1 = sb("dp1", T)
        dq1 = stg
        dqb = [ewb, dq1]
        dpb = [ewb2, dp1]
        rows = e(nc.sbuf_tensor("rows", [1, 2 * T], F32))
        onesb = sb("onesb", 1, BF16)
        ones128b = sb("ones128b", P, BF16)
        ones1 = e(nc.sbuf_tensor("ones1", [1, P], F32))
        negones1 = e(nc.sbuf_tensor("negones1", [1, P], F32))
        pv = sb("pv", 12 * 8)
        st8 = sb("st8", 9 * 8)
        stash_h = sb("stash_h", 8)
        stash_h2 = sb("stash_h2", 8)
        gate_b = sb("gate_b", 1)
        sc_b = sb("sc_b", 8)
        cvals = sb("cvals", 2)

        PV = {n: i * 8 for i, n in enumerate(vec_names)}
        PV["ew"] = 11 * 8
        SP, SQ, SO, SA, SNA, SB_, SEU, SQT, SPT = [i * 8 for i in range(9)]
        ROW_MU, ROW_A = 0, T

        pbank = [e(nc.psum_tensor(f"pb{i}", [P, T], F32)) for i in range(2)]
        pstat = [e(nc.psum_tensor(f"pst{i}", [1, T], F32)) for i in range(2)]
        pbc = [e(nc.psum_tensor(f"pbc{i}", [P, T], F32)) for i in range(2)]
        ptr = [e(nc.psum_tensor(f"ptr{i}", [P, 4 * P], F32)) for i in range(2)]

        ops = Ops(nc)
        for s in ("s_x", "s_w0", "s_w1", "s_w2", "s_t", "s_v", "s_a", "s_g", "s_g1", "s_g2", "s_gc", "s_y"):
            ops.sem(stack, s)
        EMIT = ops.emit
        TE, VE, SE, SY, GP = "tensor", "vector", "scalar", "sync", "gpsimd"

        hview = hbuf[:].rearrange("p (g t) -> p g t", g=G, t=T + 1)
        sview = scr[:].rearrange("p (g t) -> p g t", g=G, t=T + 1)

        def vgrp(buf, g, c0=0, n=T, ncols=T):
            return buf[:, g * ncols + c0: g * ncols + c0 + n]

        def scol(buf, c):
            return buf[:, c:c + 1]

        bank_free = {}
        for b in pbank + pstat + pbc + ptr:
            bank_free[id(b)] = ("s_v", 0)

        def mm_group(bankap, items, wait_extra=(), bslice=None):
            s, v = bank_free[id(bankap)]
            n = len(items)
            out = bankap[:] if bslice is None else bslice
            tv = None
            for i, (lhs, rhs) in enumerate(items):
                w = (list(wait_extra) + [(s, v)]) if i == 0 else []
                tv = EMIT(TE, (lambda lhs=lhs, rhs=rhs, i=i, n=n, out=out:
                               nc.tensor.matmul(out, lhs, rhs,
                                                start=(i == 0), stop=(i == n - 1))),
                          waits=w, inc=("s_t" if i == n - 1 else None))
            return tv

        def consumed(bankap, sem, val):
            bank_free[id(bankap)] = (sem, val)

        for f in (lambda: nc.vector.memset(onesb[:], 1.0 / D),
                  lambda: nc.vector.memset(ones128b[:], 1.0),
                  lambda: nc.vector.memset(ones1[:], 1.0),
                  lambda: nc.vector.memset(negones1[:], -1.0),
                  lambda: nc.vector.memset(stash_h[:], 0.0),
                  lambda: nc.vector.memset(stash_h2[:], 0.0),
                  lambda: nc.vector.memset(st8[:, SP:SP + 8], 0.0),
                  lambda: nc.vector.memset(st8[:, SQ:SQ + 8], 0.0),
                  lambda: nc.vector.memset(st8[:, SO:SO + 8], 0.0),
                  lambda: nc.vector.memset(ewb[:], 0.0),
                  lambda: nc.vector.memset(ewb2[:], 0.0),
                  lambda: nc.vector.memset(dq1[:], 0.0),
                  lambda: nc.vector.memset(dp1[:], 0.0),
                  lambda: nc.vector.memset(cvals[:, 0:1], EPS),
                  lambda: nc.vector.memset(cvals[:, 1:2], LN05)):
            EMIT(VE, f)

        for n in vec_names:
            EMIT(SE, (lambda n=n: nc.scalar.dma_start(
                out=pv[:, PV[n]:PV[n] + 8],
                in_=vecs_d[n].rearrange("(g p) -> p g", p=P),
                allow_slow_non_contiguous=True)), inc="s_x", inc_amt=16)
        EMIT(SE, lambda: nc.scalar.dma_start(
            out=gate_b[:], in_=gate_d.broadcast_to([P, 1])), inc="s_x", inc_amt=16)
        for i, n in enumerate(("s0", "s1", "s2")):
            EMIT(SE, (lambda i=i, n=n: nc.scalar.dma_start(
                out=sc_b[:, i:i + 1], in_=s_d[n].broadcast_to([P, 1]))),
                inc="s_x", inc_amt=16)
        x_pv = ops.val("s_x")
        EMIT(SE, lambda: nc.scalar.activation(
            st8[:, SB_:SB_ + 8], pv[:, PV["att_td"]:PV["att_td"] + 8], AF.Exp),
            waits=[("s_x", x_pv)])
        EMIT(SE, lambda: nc.scalar.activation(
            st8[:, SEU:SEU + 8], pv[:, PV["att_tf"]:PV["att_tf"] + 8], AF.Exp),
            inc="s_a")
        a_params = ops.val("s_a")
        v_w = EMIT(VE, lambda: nc.vector.tensor_scalar(
            pv[:, PV["ew"]:PV["ew"] + 8], st8[:, SB_:SB_ + 8], -1.0, None,
            op0=ALU.mult), waits=[("s_a", a_params)], inc="s_v")
        EMIT(VE, lambda: nc.vector.tensor_scalar(
            sc_b[:, 0:1], sc_b[:, 0:1], 4.0, None, op0=ALU.mult),
            waits=[("s_x", x_pv)])
        EMIT(VE, lambda: nc.vector.tensor_scalar(
            sc_b[:, 3:4], sc_b[:, 1:2], 4.0, None, op0=ALU.mult))
        EMIT(VE, lambda: nc.vector.tensor_scalar(
            sc_b[:, 4:5], sc_b[:, 2:3], 1.0, None, op0=ALU.mult))
        EMIT(VE, lambda: nc.vector.tensor_scalar(
            sc_b[:, 1:2], sc_b[:, 1:2], 2.0, None, op0=ALU.mult))
        EMIT(VE, lambda: nc.vector.tensor_scalar(
            sc_b[:, 2:3], sc_b[:, 2:3], 2.0, None, op0=ALU.mult))
        EMIT(VE, lambda: nc.vector.tensor_scalar(
            sc_b[:, 5:6], sc_b[:, 1:2], -1.0, None, op0=ALU.mult))
        EMIT(VE, lambda: nc.vector.tensor_scalar(
            sc_b[:, 6:7], sc_b[:, 4:5], -1.0, None, op0=ALU.mult), inc="s_v")

        conv = {}
        for n in wsq_names:
            conv[n] = ("s_g", EMIT(GP, (lambda n=n: nc.gpsimd.dma_start(
                out=wsq_s[n][:, :], in_=wsq_d[n][:, :])), inc="s_g", inc_amt=16))
        conv["fwk"] = ("s_g", EMIT(GP, lambda: nc.gpsimd.dma_start(
            out=fwk_s[:, :], in_=fwk_d[:, :]), inc="s_g", inc_amt=16))
        gv = None
        for oc in range(8):
            gv = EMIT(GP, (lambda oc=oc: nc.gpsimd.dma_start(
                out=fwv_s[oc, :, :], in_=fwv_d[:, oc * P:(oc + 1) * P])),
                inc="s_g", inc_amt=16)
        conv["fwv"] = ("s_g", gv)

        def emit_sign_conversions():
            slot_busy = [0, 0]
            for kc in range(G):
                s0_ = (kc % 2) * 2 * FF
                gva = EMIT(GP, (lambda kc=kc, s0_=s0_: nc.gpsimd.dma_start(
                    out=big1[:, s0_:s0_ + FF], in_=k0_d[kc * P:(kc + 1) * P, :])),
                    waits=[("s_g", slot_busy[kc % 2])], inc="s_g", inc_amt=16)
                gvb = EMIT(GP, (lambda kc=kc, s0_=s0_: nc.gpsimd.dma_start(
                    out=big1[:, s0_ + FF:s0_ + 2 * FF],
                    in_=k0_d[D + kc * P:D + (kc + 1) * P, :])),
                    inc="s_g", inc_amt=16)
                cv = EMIT(VE, (lambda s0_=s0_: nc.vector.tensor_sub(
                    big1[:, s0_:s0_ + FF], big1[:, s0_:s0_ + FF],
                    big1[:, s0_ + FF:s0_ + 2 * FF])),
                    waits=[("s_g", gvb)], inc="s_v")
                slot_busy[kc % 2] = EMIT(GP, (lambda kc=kc, s0_=s0_:
                                              nc.gpsimd.dma_start(
                    out=w0_s[kc * P:(kc + 1) * P, :], in_=big1[:, s0_:s0_ + FF])),
                    waits=[("s_v", cv)], inc="s_g", inc_amt=16)
            conv["w0"] = ("s_g", max(slot_busy))

        slab = {"i": 0, "free": [("s_v", 0), ("s_v", 0), ("s_v", 0)]}
        SY_SEMS = ("s_w0", "s_w1", "s_w2")
        GP_SEMS = ("s_g", "s_g1", "s_g2")

        def load_slab(src3, kc_n, conv_wait, queue=SY):
            i = slab["i"]
            buf = wring[i % 3]
            dst = buf[:].rearrange("p (kc m) -> p kc m", kc=kc_n)
            waits = [slab["free"][i % 3]]
            if conv_wait is not None:
                waits.append(conv_wait)
            if queue == SY:
                sem = SY_SEMS[i % 3]
                wv_ = EMIT(SY, (lambda src3=src3, dst=dst: nc.sync.dma_start(
                    out=dst, in_=src3)), waits=waits, inc=sem, inc_amt=16)
            else:
                sem = GP_SEMS[i % 3]
                wv_ = EMIT(GP, (lambda src3=src3, dst=dst: nc.gpsimd.dma_start(
                    out=dst, in_=src3)), waits=waits, inc=sem, inc_amt=16)
            slab["i"] += 1
            return buf, (sem, wv_), i

        def sq_slab(scrap, c0, width=T):
            return scrap[:, c0:c0 + width].rearrange("(kc p) m -> p kc m", p=P)

        def blk_slab(scr3, blk):
            return scr3[blk, :, :].rearrange("(kc p) m -> p kc m", p=P)

        def sq_slabs(scrap, n_oc):
            return [(sq_slab(scrap, h * T), min(4, n_oc - h * 4))
                    for h in range((n_oc + 3) // 4)]

        def proj(dst_evac, mov_fn, kc_n, slabs, conv_wait, wait_extra=(),
                 mm_width=T, queue=SY, banks=None):
            banks = banks or pbank
            oc = 0
            for src3, opb in slabs:
                buf, wv_, si = load_slab(src3, kc_n, conv_wait, queue=queue)
                last_tv = None
                for j in range(opb):
                    bank = banks[oc % len(banks)]
                    if opb == 1:
                        items = [(buf[:, kc * P:(kc + 1) * P], mov_fn(kc))
                                 for kc in range(kc_n)]
                    else:
                        items = [(buf[:, kc * T + j * P: kc * T + j * P + P],
                                  mov_fn(kc)) for kc in range(kc_n)]
                    we = (list(wait_extra) if oc == 0 else []) + [wv_]
                    tv = mm_group(bank, items, wait_extra=we,
                                  bslice=(None if mm_width == T
                                          else bank[:, 0:mm_width]))
                    last_tv = tv
                    sem, val = dst_evac(oc, bank, tv)
                    consumed(bank, sem, val)
                    oc += 1
                slab["free"][si % 3] = ("s_t", last_tv)

        def ln_emit(src, dst, dst_ncols, dst_c0, gcol, bcol, ncol=T, col0=0,
                    bc_banks=None, cast_buf=None, pre_wait=None, stats_done=None,
                    apply_halves=None):
            bcb = bc_banks or pbc
            xap = cast_buf if cast_buf is not None else xbf[:]

            def sl(b, nc_=T):
                return b[:].rearrange("p (g t) -> p g t", g=G)[:, :, col0:col0 + ncol]

            slx = xap.rearrange("p (g t) -> p g t", g=G)[:, :, col0:col0 + ncol]

            def xgrp(g):
                return xap[:, g * T + col0: g * T + col0 + ncol]

            def sld(b):
                if dst_ncols == T:
                    return sl(b)
                return b[:].rearrange("p (g t) -> p g t", g=G, t=T + 1)[
                    :, :, col0:col0 + ncol]
            if stats_done is None:
                cv = EMIT(VE, lambda: nc.vector.tensor_copy(slx, sl(src)),
                          waits=([pre_wait] if pre_wait else []), inc="s_v")
                tv0 = mm_group(pstat[0],
                               [(onesb[:], xgrp(g)) for g in range(G)],
                               wait_extra=[("s_v", cv)], bslice=pstat[0][:, 0:ncol])
                cv = EMIT(VE, lambda: nc.vector.tensor_mul(slx, slx, slx),
                          waits=[("s_t", tv0)], inc="s_v")
                tv1 = mm_group(pstat[1],
                               [(onesb[:], xgrp(g)) for g in range(G)],
                               wait_extra=[("s_v", cv)], bslice=pstat[1][:, 0:ncol])
            else:
                tv0, tv1 = stats_done
            rv = EMIT(VE, lambda: nc.vector.tensor_copy(
                rows[:, ROW_MU:ROW_MU + ncol], pstat[0][:, 0:ncol]),
                waits=[("s_t", tv0)], inc="s_v")
            consumed(pstat[0], "s_v", rv)
            rv = EMIT(VE, lambda: nc.vector.tensor_mul(
                rows[:, ROW_A:ROW_A + ncol], rows[:, ROW_MU:ROW_MU + ncol],
                rows[:, ROW_MU:ROW_MU + ncol]), inc="s_v")
            rv = EMIT(VE, lambda: nc.vector.tensor_sub(
                rows[:, ROW_A:ROW_A + ncol], pstat[1][:, 0:ncol],
                rows[:, ROW_A:ROW_A + ncol]), waits=[("s_t", tv1)], inc="s_v")
            consumed(pstat[1], "s_v", rv)
            av = EMIT(SE, lambda: nc.scalar.activation(
                rows[:, ROW_A:ROW_A + ncol], rows[:, ROW_A:ROW_A + ncol],
                AF.Ln, bias=cvals[0:1, 0:1]), waits=[("s_v", rv)])
            av = EMIT(SE, lambda: nc.scalar.activation(
                rows[:, ROW_A:ROW_A + ncol], rows[:, ROW_A:ROW_A + ncol],
                AF.Exp, scale=-0.5), inc="s_a")
            rv = EMIT(VE, lambda: nc.vector.tensor_mul(
                rows[:, ROW_MU:ROW_MU + ncol], rows[:, ROW_MU:ROW_MU + ncol],
                rows[:, ROW_A:ROW_A + ncol]), waits=[("s_a", av)], inc="s_v")
            tvb1 = None
            mm_group(bcb[0], [(ones1[:], rows[:, ROW_A:ROW_A + ncol])],
                     wait_extra=[("s_v", rv)], bslice=bcb[0][:, 0:ncol])
            tvb1 = mm_group(bcb[1], [(negones1[:], rows[:, ROW_MU:ROW_MU + ncol])],
                            bslice=bcb[1][:, 0:ncol])
            rstd_b = bcb[0][:, 0:ncol].unsqueeze(1).broadcast_to([P, G, ncol])
            shift_b = bcb[1][:, 0:ncol].unsqueeze(1).broadcast_to([P, G, ncol])
            if gcol is None:
                dsl = sld(dst)
                if apply_halves is not None:
                    ev = None
                    for h in (0, 1):
                        EMIT(VE, (lambda h=h: nc.vector.tensor_mul(
                            dsl[:, 4 * h:4 * h + 4, :],
                            sl(src)[:, 4 * h:4 * h + 4, :],
                            rstd_b[:, 4 * h:4 * h + 4, :])),
                            waits=([("s_t", tvb1)] if h == 0 else []), inc="s_v")
                        ev = EMIT(VE, (lambda h=h: nc.vector.tensor_add(
                            dsl[:, 4 * h:4 * h + 4, :],
                            dsl[:, 4 * h:4 * h + 4, :],
                            shift_b[:, 4 * h:4 * h + 4, :])), inc="s_v")
                        apply_halves.append(ev)
                    consumed(bcb[0], "s_v", ev)
                    consumed(bcb[1], "s_v", ev)
                    return ("s_v", ev)
                ev = EMIT(VE, lambda: nc.vector.tensor_mul(dsl, sl(src), rstd_b),
                          waits=[("s_t", tvb1)], inc="s_v")
                ev = EMIT(VE, lambda: nc.vector.tensor_add(dsl, dsl, shift_b),
                          inc="s_v")
                consumed(bcb[0], "s_v", ev)
                consumed(bcb[1], "s_v", ev)
                return ("s_v", ev)
            ssl = sview[:, :, col0:col0 + ncol]
            ev = EMIT(VE, lambda: nc.vector.tensor_mul(ssl, sl(src), rstd_b),
                      waits=[("s_t", tvb1)], inc="s_v")
            ev = EMIT(VE, lambda: nc.vector.tensor_add(ssl, ssl, shift_b), inc="s_v")
            consumed(bcb[0], "s_v", ev)
            consumed(bcb[1], "s_v", ev)
            avs = None
            for g in range(G):
                avs = EMIT(VE, (lambda g=g: nc.vector.tensor_scalar(
                    vgrp(dst, g, dst_c0 + col0, ncol, dst_ncols),
                    vgrp(scr, g, col0, ncol, T + 1),
                    pv[:, gcol + g:gcol + g + 1],
                    pv[:, bcol + g:bcol + g + 1],
                    op0=ALU.mult, op1=ALU.add)), inc="s_v")
            return ("s_v", avs)

        stats_acc = {}

        stats_pend = {}

        def stats_oc(oc, srcap, xb, xb2, wait=None):
            w = [wait] if wait else []
            cv1 = EMIT(SE, (lambda oc=oc, srcap=srcap, xb=xb: nc.scalar.copy(
                xb[:, oc * T:(oc + 1) * T], srcap)), waits=w, inc="s_a")
            cv2 = EMIT(SE, (lambda oc=oc, srcap=srcap, xb2=xb2: nc.scalar.activation(
                xb2[:, oc * T:(oc + 1) * T], srcap, AF.Square)), inc="s_a")
            stats_pend[oc] = (cv1, cv2)

        def stats_flush(xb, xb2):
            tv0 = tv1 = None
            for oc in range(G):
                cv1, cv2 = stats_pend[oc]
                s_, v_ = bank_free[id(pstat[0])]
                tv0 = EMIT(TE, (lambda oc=oc, xb=xb: nc.tensor.matmul(
                    pstat[0][:], onesb[:], xb[:, oc * T:(oc + 1) * T],
                    start=(oc == 0), stop=(oc == G - 1))),
                    waits=[("s_a", cv1)] + ([(s_, v_)] if oc == 0 else []),
                    inc=("s_t" if oc == G - 1 else None))
                s_, v_ = bank_free[id(pstat[1])]
                tv1 = EMIT(TE, (lambda oc=oc, xb2=xb2: nc.tensor.matmul(
                    pstat[1][:], onesb[:], xb2[:, oc * T:(oc + 1) * T],
                    start=(oc == 0), stop=(oc == G - 1))),
                    waits=[("s_a", cv2)] + ([(s_, v_)] if oc == 0 else []),
                    inc=("s_t" if oc == G - 1 else None))
            stats_acc["tv"] = (tv0, tv1)

        def mix(dst, mcol, col0=0, ncol=T):
            r = None
            for g in range(G):
                r = EMIT(VE, (lambda g=g, dst=dst, mcol=mcol:
                              nc.vector.scalar_tensor_tensor(
                                  vgrp(dst, g, col0, ncol),
                                  vgrp(scr, g, col0, ncol, T + 1),
                                  pv[:, mcol + g:mcol + g + 1],
                                  hview[:, g, col0:col0 + ncol],
                                  op0=ALU.mult, op1=ALU.add)), inc="s_v")
            return r

        def tap_store(tn, buf, wait):
            if tn not in taps:
                return
            yv = EMIT(SE, (lambda tn=tn, buf=buf: nc.scalar.dma_start(
                out=tap_d[tn], in_=buf[:])), waits=[wait], inc="s_y", inc_amt=16)
            EMIT(VE, lambda: nc.vector.engine_nop(), waits=[("s_y", yv)], inc="s_v")

        x0_free = [("s_v", 0)]
        x_ld = [0]
        y_store = [0]
        y_store_g = [0] * G
        m_halves = []
        kmax = [0]
        y_done = [None] * G
        v_done = [0] * G
        decay_free = {(s, i): ("s_v", 0) for s in ("q", "p") for i in (0, 1)}

        def phase_A(t_idx):
            ops.phase = "A_in"
            tok0 = t_idx * T
            lv = EMIT(SE, (lambda tok0=tok0: nc.scalar.dma_start(
                out=x0[:].rearrange("p (g t) -> p g t", g=G),
                in_=xin_d.rearrange("(g p) n -> p g n", p=P)[:, :, tok0:tok0 + T])),
                waits=[x0_free[0]], inc="s_x", inc_amt=16)
            x_ld[0] = lv

        mix_done = {}
        ijk_state = {}

        def phase_B1(t_idx, with_r, bc_banks=None):
            ops.phase = "B_ln1"
            EMIT(VE, lambda: nc.vector.tensor_copy(
                hview[:, :, 0:1], stash_h[:].rearrange("p g -> p g ()")))
            sem_h = ln_emit(x0, hbuf, T + 1, 1, PV["ln_att_g"], PV["ln_att_b"],
                            bc_banks=bc_banks, pre_wait=("s_x", x_ld[0]))
            sv = EMIT(VE, lambda: nc.vector.tensor_copy(
                stash_h[:].rearrange("p g -> p g ()"), hview[:, :, T:T + 1]),
                waits=[sem_h], inc="s_v")
            tap_store("h", hbuf, ("s_v", sv))
            EMIT(VE, lambda: nc.vector.tensor_sub(
                sview[:, :, 0:T], hview[:, :, 1:T + 1], hview[:, :, 0:T]),
                waits=[sem_h], inc="s_v")
            x0_free[0] = ("s_v", ops.val("s_v"))
            mix_done["v"] = mix(xxb, PV["amix_v"])
            mix_done["k"] = mix(xxa, PV["amix_k"])
            mix_done["r"] = mix(xbf, PV["amix_r"]) if with_r else 0

        def phase_C(t_idx, with_r, full):
            ops.phase = "C_kvr"
            xxv_v = mix_done["v"]
            xxk_v = mix_done["k"]
            xxr_v = mix_done["r"]

            sq_done = [0] * G
            qfill = [0] * G
            pfill = [0] * G
            kva = [0] * G

            def wkv_q(g):
                c0 = g * (T + 1)
                EMIT(VE, (lambda g=g: nc.vector.tensor_max(
                    st8[:, SA + g:SA + g + 1], st8[:, SA + g:SA + g + 1],
                    st8[:, SO + g:SO + g + 1])))
                EMIT(VE, (lambda g=g: nc.vector.tensor_sub(
                    st8[:, SB_ + g:SB_ + g + 1], st8[:, SO + g:SO + g + 1],
                    st8[:, SA + g:SA + g + 1])))
                EMIT(VE, (lambda g=g: nc.vector.tensor_copy(
                    st8[:, SO + g:SO + g + 1], st8[:, SA + g:SA + g + 1])))
                av = EMIT(VE, (lambda g=g: nc.vector.tensor_scalar(
                    st8[:, SNA + g:SNA + g + 1], st8[:, SA + g:SA + g + 1],
                    -1.0, None, op0=ALU.mult)), inc="s_v")
                bv = EMIT(SE, (lambda g=g: nc.scalar.activation(
                    st8[:, SB_ + g:SB_ + g + 1], st8[:, SB_ + g:SB_ + g + 1],
                    AF.Exp)), waits=[("s_v", av), ("s_a", a_params)], inc="s_a")
                kva[g] = EMIT(SE, (lambda g=g: nc.scalar.activation(
                    vgrp(kvs, g), vgrp(kvs, g), AF.Exp,
                    bias=st8[:, SNA + g:SNA + g + 1])), inc="s_a")
                qtv = EMIT(VE, (lambda g=g: nc.vector.tensor_mul(
                    st8[:, SQT + g:SQT + g + 1], st8[:, SQ + g:SQ + g + 1],
                    st8[:, SB_ + g:SB_ + g + 1])),
                    waits=[("s_a", kva[g])], inc="s_v")
                EMIT(VE, (lambda g=g, c0=c0: nc.vector.tensor_copy(
                    scr[:, c0:c0 + 1], st8[:, SQT + g:SQT + g + 1])),
                    waits=[("s_v", qtv), ("s_y", y_store_g[g])])
                sqv = EMIT(VE, (lambda g=g, c0=c0: nc.vector.tensor_tensor_scan(
                    vgrp(scr, g, 1, T, T + 1), dqb[g % 2][:], vgrp(kvs, g),
                    st8[:, SQT + g:SQT + g + 1], op0=ALU.mult, op1=ALU.add)),
                    waits=[("s_a", qfill[g])], inc="s_v")
                EMIT(VE, (lambda g=g, c0=c0: nc.vector.tensor_copy(
                    st8[:, SQ + g:SQ + g + 1], scr[:, c0 + T:c0 + T + 1])),
                    waits=[("s_v", sqv)], inc="s_v")
                decay_free[("q", g % 2)] = ("s_v", sqv)
                sq_done[g] = sqv

            def wkv_p(g):
                c0 = g * (T + 1)
                EMIT(VE, (lambda g=g: nc.vector.tensor_mul(
                    vgrp(vvs, g), vgrp(kvs, g), vgrp(vvs, g))),
                    waits=[("s_a", v_done[g]), ("s_a", kva[g])], inc="s_v")
                ln_a = None
                if full:
                    dn = EMIT(VE, (lambda g=g: nc.vector.scalar_tensor_tensor(
                        vgrp(kvs, g), vgrp(kvs, g),
                        st8[:, SEU + g:SEU + g + 1],
                        vgrp(scr, g, 0, T, T + 1), op0=ALU.mult, op1=ALU.add)),
                        waits=[("s_v", sq_done[g])], inc="s_v")
                    EMIT(SE, (lambda g=g: nc.scalar.activation(
                        vgrp(kvs, g), vgrp(kvs, g), AF.Ln)),
                        waits=[("s_v", dn)])
                    ln_a = EMIT(SE, (lambda g=g: nc.scalar.activation(
                        vgrp(kvs, g), vgrp(kvs, g), AF.Exp, scale=-1.0,
                        bias=cvals[:, 1:2])), inc="s_a")
                ptv = EMIT(VE, (lambda g=g: nc.vector.tensor_mul(
                    st8[:, SPT + g:SPT + g + 1], st8[:, SP + g:SP + g + 1],
                    st8[:, SB_ + g:SB_ + g + 1])), inc="s_v")
                EMIT(VE, (lambda g=g, c0=c0: nc.vector.tensor_copy(
                    scr[:, c0:c0 + 1], st8[:, SPT + g:SPT + g + 1])),
                    waits=[("s_v", ptv)])
                spv = EMIT(VE, (lambda g=g, c0=c0: nc.vector.tensor_tensor_scan(
                    vgrp(scr, g, 1, T, T + 1), dpb[g % 2][:], vgrp(vvs, g),
                    st8[:, SPT + g:SPT + g + 1], op0=ALU.mult, op1=ALU.add)),
                    waits=[("s_a", pfill[g])], inc="s_v")
                EMIT(VE, (lambda g=g, c0=c0: nc.vector.tensor_copy(
                    st8[:, SP + g:SP + g + 1], scr[:, c0 + T:c0 + T + 1])),
                    waits=[("s_v", spv)], inc="s_v")
                decay_free[("p", g % 2)] = ("s_v", spv)
                if full:
                    EMIT(VE, (lambda g=g: nc.vector.scalar_tensor_tensor(
                        vgrp(vvs, g), vgrp(vvs, g), st8[:, SEU + g:SEU + g + 1],
                        vgrp(scr, g, 0, T, T + 1), op0=ALU.mult, op1=ALU.add)))
                    yv = EMIT(VE, (lambda g=g: nc.vector.tensor_mul(
                        vgrp(vvs, g), vgrp(vvs, g), vgrp(kvs, g))),
                        waits=([("s_a", ln_a)] if full else []), inc="s_v")
                    y_done[g] = ("s_v", yv)
                else:
                    y_done[g] = ("s_v", ops.val("s_v"))

            def k_evac(oc, bank, tv):
                ev = EMIT(SE, (lambda oc=oc, bank=bank: nc.scalar.copy(
                    vgrp(kvs, oc), bank[:])),
                    waits=[("s_t", tv)] + ([y_done[oc]] if y_done[oc] else []),
                    inc="s_a")
                qfill[oc] = EMIT(SE, (lambda oc=oc: nc.scalar.activation(
                    dqb[oc % 2][:], dqb[oc % 2][:], AF.Exp, scale=0.0,
                    bias=pv[:, PV["ew"] + oc:PV["ew"] + oc + 1])),
                    waits=[decay_free[("q", oc % 2)], ("s_v", v_w)], inc="s_a")
                EMIT(VE, (lambda oc=oc: nc.vector.reduce_max(
                    st8[:, SA + oc:SA + oc + 1], vgrp(kvs, oc),
                    axis=mybir.AxisListType.X)), waits=[("s_a", ev)], inc="s_v")
                if oc >= 1:
                    wkv_q(oc - 1)
                return "s_a", ev

            proj(k_evac, lambda kc: xxa[:, kc * T:(kc + 1) * T], G,
                 sq_slabs(wsq_s["wk"], G), conv["wk"], wait_extra=[("s_v", xxk_v)],
                 banks=pbank + pbc)
            wkv_q(G - 1)

            def v_evac(oc, bank, tv):
                ev = EMIT(SE, (lambda oc=oc, bank=bank: nc.scalar.copy(
                    vgrp(vvs, oc), bank[:])),
                    waits=[("s_t", tv)] + ([y_done[oc]] if y_done[oc] else []),
                    inc="s_a")
                pfill[oc] = EMIT(SE, (lambda oc=oc: nc.scalar.activation(
                    dpb[oc % 2][:], dpb[oc % 2][:], AF.Exp, scale=0.0,
                    bias=pv[:, PV["ew"] + oc:PV["ew"] + oc + 1])),
                    waits=[decay_free[("p", oc % 2)], ("s_v", v_w)], inc="s_a")
                v_done[oc] = ev
                wkv_p(oc)
                return "s_a", ev

            proj(v_evac, lambda kc: xxb[:, kc * T:(kc + 1) * T], G,
                 sq_slabs(wsq_s["wv"], G), conv["wv"], wait_extra=[("s_v", xxv_v)],
                 banks=pbank + pbc)

            if with_r:
                def r_evac(oc, bank, tv):
                    av = EMIT(SE, (lambda oc=oc, bank=bank: nc.scalar.activation(
                        vgrp(hbuf, oc, 1, T, T + 1), bank[:], AF.Tanh, scale=0.5)),
                        waits=[("s_t", tv)], inc="s_a")
                    EMIT(VE, (lambda oc=oc: nc.vector.scalar_tensor_tensor(
                        vgrp(xxa, oc), hview[:, oc, 1:T + 1], 1.0,
                        vgrp(vvs, oc), op0=ALU.add, op1=ALU.mult)),
                        waits=[("s_a", av), y_done[oc]], inc="s_v")
                    return "s_a", av

                proj(r_evac, lambda kc: xbf[:, kc * T:(kc + 1) * T], G,
                     sq_slabs(wsq_s["wr"], G), conv["wr"],
                     wait_extra=[("s_v", xxr_v)], banks=pbank + pbc)
            return ops.val("s_v")

        def rw_emit(col0=0, ncol=T, wait_extra=()):
            r = None
            for g in range(G):
                r = EMIT(VE, (lambda g=g: nc.vector.scalar_tensor_tensor(
                    vgrp(xxa, g, col0, ncol),
                    hview[:, g, 1 + col0:1 + col0 + ncol],
                    1.0, vgrp(vvs, g, col0, ncol), op0=ALU.add, op1=ALU.mult)),
                    waits=(list(wait_extra) if g == 0 else []), inc="s_v")
            return r

        def phase_E(rw_v):
            ops.phase = "E_wo"

            def wo_evac(oc, bank, tv):
                ev = EMIT(VE, (lambda oc=oc, bank=bank: nc.vector.tensor_add(
                    vgrp(vvs, oc), vgrp(x0, oc), bank[:])),
                    waits=[("s_t", tv)], inc="s_v")
                stats_oc(oc, vgrp(vvs, oc), xbf, xxb, wait=("s_v", ev))
                return "s_v", ev

            proj(wo_evac, lambda kc: xxa[:, kc * T:(kc + 1) * T], G,
                 sq_slabs(wsq_s["wo"], G), conv["wo"], wait_extra=[("s_v", rw_v)],
                 banks=pbank + pbc)
            stats_flush(xbf, xxb)

        def phase_FG():
            ops.phase = "F_ln2"
            EMIT(VE, lambda: nc.vector.tensor_copy(
                hview[:, :, 0:1], stash_h2[:].rearrange("p g -> p g ()")))
            sem_h2 = ln_emit(vvs, hbuf, T + 1, 1, PV["ln_ffn_g"], PV["ln_ffn_b"],
                             stats_done=stats_acc["tv"])
            sv = EMIT(VE, lambda: nc.vector.tensor_copy(
                stash_h2[:].rearrange("p g -> p g ()"), hview[:, :, T:T + 1]),
                waits=[sem_h2], inc="s_v")
            tap_store("h2", hbuf, ("s_v", sv))
            EMIT(VE, lambda: nc.vector.tensor_sub(
                sview[:, :, 0:T], hview[:, :, 1:T + 1], hview[:, :, 0:T]),
                waits=[sem_h2], inc="s_v")
            ops.phase = "G_ffn"
            xxk2_v = mix(xxa, PV["fmix_k"])
            xxr2_v = mix(xxb, PV["fmix_r"])
            def kf_evac(oc, bank, tv):
                rv = EMIT(VE, (lambda bank=bank: nc.vector.tensor_scalar_max(
                    stg[:], bank[:], 0.0)), waits=[("s_t", tv)], inc="s_v")
                EMIT(VE, (lambda oc=oc: nc.vector.tensor_mul(
                    vgrp(big1, oc), stg[:], stg[:])),
                    waits=[conv["w0"]], inc="s_v")
                return "s_v", rv

            proj(kf_evac, lambda kc: xxa[:, kc * T:(kc + 1) * T], G,
                 sq_slabs(fwk_s, FG), conv["fwk"], wait_extra=[("s_v", xxk2_v)],
                 banks=pbank + pbc)

            def rf_evac(oc, bank, tv):
                av = EMIT(SE, (lambda oc=oc, bank=bank: nc.scalar.activation(
                    vgrp(hbuf, oc, 1, T, T + 1), bank[:], AF.Tanh, scale=0.5)),
                    waits=[("s_t", tv)], inc="s_a")
                return "s_a", av

            proj(rf_evac, lambda kc: xxb[:, kc * T:(kc + 1) * T], G,
                 sq_slabs(wsq_s["fwr"], G), conv["fwr"],
                 wait_extra=[("s_v", xxr2_v)], banks=pbank + pbc)
            return ops.val("s_a")

        def phase_H(rf_a):
            ops.phase = "H_fwv"
            hv0 = ops.val("s_v")

            def fwv_evac(oc, bank, tv):
                h = oc % 2
                dst = stg[:] if h == 0 else ewb2[:]
                EMIT(VE, (lambda oc=oc, bank=bank, dst=dst:
                          nc.vector.scalar_tensor_tensor(
                              dst, hview[:, oc, 1:T + 1], 1.0, bank[:],
                              op0=ALU.add, op1=ALU.mult)),
                     waits=[("s_t", tv), ("s_a", rf_a), ("s_v", hv0)])
                EMIT(VE, (lambda oc=oc, dst=dst: nc.vector.scalar_tensor_tensor(
                    vgrp(vvs, oc), dst, 0.5, vgrp(vvs, oc),
                    op0=ALU.mult, op1=ALU.add)))
                ev = EMIT(VE, (lambda oc=oc: nc.vector.tensor_add(
                    vgrp(x0, oc), vgrp(x0, oc), vgrp(vvs, oc))), inc="s_v")
                stats_oc(oc, vgrp(x0, oc), xbf, xxa, wait=("s_v", ev))
                return "s_v", ev

            proj(fwv_evac, lambda kc: big1[:, kc * T:(kc + 1) * T], FG,
                 [(blk_slab(fwv_s, oc), 1) for oc in range(8)], conv["fwv"],
                 banks=pbank + pbc)
            stats_flush(xbf, xxa)
            decay_free[("p", 0)] = ("s_v", ops.val("s_v"))
            decay_free[("q", 1)] = ("s_v", ops.val("s_v"))

        def phase_IJK(t_idx):
            ops.phase = "I_pln1"
            ln_emit(x0, kvs, T, 0, None, None, stats_done=stats_acc["tv"])
            x0_free[0] = ("s_v", ops.val("s_v"))
            cv = EMIT(VE, lambda: nc.vector.tensor_copy(xbf[:], kvs[:]), inc="s_v")
            tap_store("x4", kvs, ("s_v", cv))

            def h1_evac(oc, bank, tv):
                av = EMIT(VE, (lambda oc=oc, bank=bank: nc.vector.tensor_scalar(
                    vgrp(big1, oc), bank[:], sc_b[:, 0:1], 0.0,
                    op0=ALU.mult, op1=ALU.max)),
                    waits=[("s_t", tv)], inc="s_v")
                return "s_v", av

            ops.phase = "J_w0"
            proj(h1_evac, lambda kc: xbf[:, kc * T:(kc + 1) * T], G,
                 sq_slabs(w0_s, FG), conv["w0"], wait_extra=[("s_v", cv)],
                 banks=pbank + pbc)
            if "h1s" in taps:
                tap_via_f32("h1s", big1, FG)

            ops.phase = "K_w1"
            h1_a = ops.val("s_v")
            tvS = mm_group(pbc[0],
                           [(ones128b[:], vgrp(big1, kc)) for kc in range(FG)],
                           wait_extra=[("s_v", h1_a)])
            bcv = EMIT(VE, lambda: nc.vector.tensor_scalar(
                ewb[:], pbc[0][:], sc_b[:, 5:6], None, op0=ALU.mult),
                waits=[("s_t", tvS)], inc="s_v")
            consumed(pbc[0], "s_v", bcv)
            def h2s_evac(oc, bank, tv):
                h = oc % 2
                dst = stg[:] if h == 0 else ewb2[:]
                EMIT(VE, (lambda bank=bank, dst=dst:
                          nc.vector.scalar_tensor_tensor(
                              dst, bank[:], sc_b[:, 3:4], ewb[:],
                              op0=ALU.mult, op1=ALU.add)),
                     waits=[("s_t", tv), ("s_v", bcv)])
                ev = EMIT(VE, (lambda oc=oc, dst=dst: nc.vector.tensor_scalar_max(
                    vgrp(big2, oc), dst, 0.0)), inc="s_v")
                return "s_v", ev

            proj(h2s_evac, lambda kc: big1[:, kc * T:(kc + 1) * T], FG,
                 [(k1_d[mc].rearrange("p (kc m) -> p kc m", kc=32), 1)
                  for mc in range(32)], None, queue=GP,
                 banks=[pbank[0], pbank[1], pbc[1]])
            ijk_state["la"] = ("s_v", ops.val("s_v"))
            ijk_state["lastk"] = ops.val("s_v")
            if "h2s" in taps:
                tap_via_f32("h2s", big2, FG)

        def phase_LM(t_idx):
            ops.phase = "L_w2"
            tvS2 = mm_group(pbc[1],
                            [(ones128b[:], vgrp(big2, kc)) for kc in range(FG)],
                            wait_extra=[ijk_state["la"]])
            bcv2 = EMIT(VE, lambda: nc.vector.tensor_scalar(
                ewb[:], pbc[1][:], sc_b[:, 6:7], None, op0=ALU.mult),
                waits=[("s_t", tvS2), ("s_v", ijk_state["lastk"])], inc="s_v")
            consumed(pbc[1], "s_v", bcv2)

            def x5_evac(oc, bank, tv):
                EMIT(VE, (lambda oc=oc, bank=bank:
                          nc.vector.scalar_tensor_tensor(
                              vgrp(vvs, oc), bank[:], sc_b[:, 2:3],
                              vgrp(kvs, oc), op0=ALU.mult, op1=ALU.add)),
                     waits=[("s_t", tv), ("s_v", bcv2)])
                ev = EMIT(VE, (lambda oc=oc: nc.vector.tensor_add(
                    vgrp(vvs, oc), vgrp(vvs, oc), ewb[:])), inc="s_v")
                stats_oc(oc, vgrp(vvs, oc), big1,
                         big1[:, G * T:2 * G * T].rearrange('p c -> p c'),
                         wait=("s_v", ev))
                return "s_v", ev

            proj(x5_evac, lambda kc: big2[:, kc * T:(kc + 1) * T], FG,
                 [(k2_d[oc].rearrange("p (kc m) -> p kc m", kc=32), 1)
                  for oc in range(8)], None, queue=GP,
                 banks=[pbank[0], pbank[1], pbc[0]])
            stats_flush(big1, big1[:, G * T:2 * G * T])
            decay_free[("q", 0)] = ("s_v", ops.val("s_v"))
            ops.phase = "M_pln2"
            m_halves.clear()
            ln_emit(vvs, scr, T + 1, 0, None, None, cast_buf=big1[:, 0:G * T],
                    stats_done=stats_acc["tv"], apply_halves=m_halves)

        def tap_via_f32(tn, bfbuf, ng):
            for part in range(ng // G):
                cv = EMIT(VE, (lambda part=part: nc.vector.tensor_copy(
                    scr[:, 0:G * T].rearrange("p (g t) -> p g t", g=G),
                    bfbuf[:, part * G * T:(part + 1) * G * T].rearrange(
                        "p (g t) -> p g t", g=G))), inc="s_v")
                yv = EMIT(SE, (lambda tn=tn, part=part: nc.scalar.dma_start(
                    out=tap_d[tn][:, part * G * T:(part + 1) * G * T],
                    in_=scr[:, 0:G * T])), waits=[("s_v", cv)],
                    inc="s_y", inc_amt=16)
                EMIT(VE, lambda: nc.vector.engine_nop(), waits=[("s_y", yv)],
                     inc="s_v")

        def phase_N(t_idx):
            ops.phase = "N_out"
            t0c = (t_idx - n_warm) * T
            for g in range(G):
                mv = m_halves[g // 4]
                yv = EMIT(SE, (lambda g=g, t0c=t0c: nc.scalar.dma_start(
                    out=y_d[g * P:(g + 1) * P, t0c:t0c + T],
                    in_=vgrp(scr, g, 0, T, T + 1))),
                    waits=[("s_v", mv)], inc="s_y", inc_amt=16)
                y_store_g[g] = yv
            y_store[0] = ops.val("s_y")

        def bridge():
            ops.phase = "X_bridge"
            EMIT(VE, lambda: nc.vector.engine_nop(),
                 waits=[("s_gc", ops.val("s_gc"))], inc="s_v")
            c0 = T - P
            EMIT(VE, lambda: nc.vector.tensor_sub(
                sview[:, :, c0:T], hview[:, :, c0 + 1:T + 1], hview[:, :, c0:T]),
                inc="s_v")
            xxr_v = mix(xbf, PV["amix_r"], col0=c0, ncol=P)

            def r_evac(oc, bank, tv):
                av = EMIT(SE, (lambda oc=oc, bank=bank: nc.scalar.activation(
                    vgrp(hbuf, oc, 1 + c0, P, T + 1), bank[:, 0:P],
                    AF.Tanh, scale=0.5)), waits=[("s_t", tv)], inc="s_a")
                return "s_a", av

            proj(r_evac, lambda kc: xbf[:, kc * T + c0: kc * T + c0 + P], G,
                 sq_slabs(wsq_s["wr"], G), conv["wr"],
                 wait_extra=[("s_v", xxr_v)], mm_width=P)
            rwv = rw_emit(col0=c0, ncol=P, wait_extra=[("s_a", ops.val("s_a"))])

            def wo_evac(oc, bank, tv):
                ev = EMIT(VE, (lambda oc=oc, bank=bank: nc.vector.tensor_add(
                    vgrp(vvs, oc, c0, P), vgrp(x0, oc, c0, P), bank[:, 0:P])),
                    waits=[("s_t", tv)], inc="s_v")
                return "s_v", ev

            proj(wo_evac, lambda kc: xxa[:, kc * T + c0: kc * T + c0 + P], G,
                 sq_slabs(wsq_s["wo"], G), conv["wo"],
                 wait_extra=[("s_v", rwv)], mm_width=P)
            ln_emit(vvs, hbuf, T + 1, 1, PV["ln_ffn_g"], PV["ln_ffn_b"],
                    ncol=P, col0=c0)
            EMIT(VE, lambda: nc.vector.tensor_copy(
                stash_h2[:].rearrange("p g -> p g ()"), hview[:, :, T:T + 1]),
                waits=[("s_a", ops.val("s_a"))], inc="s_v")
            gb8 = gate_b[:, 0:1].broadcast_to([P, 8])
            for off in (SP, SQ, SO):
                EMIT(VE, (lambda off=off: nc.vector.tensor_mul(
                    st8[:, off:off + 8], st8[:, off:off + 8], gb8)))
            EMIT(VE, lambda: nc.vector.tensor_mul(stash_h[:], stash_h[:], gb8))
            EMIT(VE, lambda: nc.vector.tensor_mul(stash_h2[:], stash_h2[:], gb8),
                 inc="s_v")

        if n_warm == 0:
            emit_sign_conversions()
        hoisted = [False]
        for t_idx in range(NTILE):
            is_main = t_idx >= n_warm
            if not hoisted[0]:
                phase_A(t_idx)
                phase_B1(t_idx, with_r=is_main)
            hoisted[0] = False
            full = is_main or t_idx == n_warm - 1
            rwv = phase_C(t_idx, with_r=is_main, full=full)
            if t_idx == 0 and not is_main:
                ops.phase = "S_conv"
                emit_sign_conversions()
            if is_main:
                tap_store("wkv", vvs, ("s_v", ops.val("s_v")))
                phase_E(rwv)
                tap_store("x1", vvs, ("s_v", ops.val("s_v")))
                rf_a = phase_FG()
                phase_H(rf_a)
                tap_store("x2", vvs, ("s_v", ops.val("s_v")))
                phase_IJK(t_idx)
                if t_idx + 1 < NTILE:
                    phase_A(t_idx + 1)
                    phase_B1(t_idx + 1, with_r=True, bc_banks=ptr)
                    hoisted[0] = True
                phase_LM(t_idx)
                phase_N(t_idx)
            elif t_idx == n_warm - 1:
                bridge()

        EMIT(SY, lambda: None, waits=[("s_y", ops.val("s_y"))])

        def make_runner(lst):
            def run(eng_obj):
                cur = None
                sid = None
                for w, fn, inc, phase in lst:
                    for h, v in w:
                        eng_obj.wait_ge(h, v)
                    if fn is not None:
                        if phase != cur:
                            if cur is not None:
                                nc.leave_named_scope(cur, sid, False)
                            sid, _ = nc.enter_named_scope(phase, False)
                            cur = phase
                        instr = fn()
                        if inc is not None:
                            instr.then_inc(inc[0], inc[2])
                if cur is not None:
                    nc.leave_named_scope(cur, sid, False)
            return run

        runners = {k: make_runner(v) for k, v in ops.lists.items()}
        block = e(nc.Block())

        @block.tensor
        def _(engine):
            runners["tensor"](engine)

        @block.vector
        def _(engine):
            runners["vector"](engine)

        @block.scalar
        def _(engine):
            runners["scalar"](engine)

        @block.sync
        def _(engine):
            runners["sync"](engine)

        @block.gpsimd
        def _(engine):
            runners["gpsimd"](engine)

    return nc


def _shard_inputs(inputs, n_warm=NWARM, n_main=NMAIN):
    x = np.asarray(inputs["L0_attn_x"])
    W = n_warm * T
    M = n_main * T
    in_maps = []
    for c in range(8):
        seq, ch = c // 4, c % 4
        start = ch * M
        xs = x[seq]
        if ch == 0:
            if W > 0:
                xw = np.concatenate([xs[0:W], xs[0:M]], axis=0)
            else:
                xw = xs[0:M]
            gate = np.zeros((1, 1), np.float32)
        else:
            xw = xs[start - W:start + M]
            gate = np.ones((1, 1), np.float32)
        xw = np.ascontiguousarray(xw.T)
        m = {
            "xin": xw, "gate": gate,
            "ln_att_g": inputs["L0_attn_ln_att_g"],
            "ln_att_b": inputs["L0_attn_ln_att_b"],
            "ln_ffn_g": inputs["L0_attn_ln_ffn_g"],
            "ln_ffn_b": inputs["L0_attn_ln_ffn_b"],
            "att_td": inputs["L0_attn_att_time_decay"],
            "att_tf": inputs["L0_attn_att_time_first"],
            "amix_k": inputs["L0_attn_att_mix_k"],
            "amix_v": inputs["L0_attn_att_mix_v"],
            "amix_r": inputs["L0_attn_att_mix_r"],
            "fmix_k": inputs["L0_attn_ffn_mix_k"],
            "fmix_r": inputs["L0_attn_ffn_mix_r"],
            "wk": inputs["L0_attn_att_wk"], "wv": inputs["L0_attn_att_wv"],
            "wr": inputs["L0_attn_att_wr"], "wo": inputs["L0_attn_att_wo"],
            "fwk": inputs["L0_attn_ffn_wk"], "fwv": inputs["L0_attn_ffn_wv"],
            "fwr": inputs["L0_attn_ffn_wr"],
            "k0": np.asarray(inputs["L0_ffn_k0"]).view(np.uint8),
            "k1": np.asarray(inputs["L0_ffn_k1"]).view(np.uint8)
                .reshape(32, P, 32, P).transpose(2, 1, 0, 3)
                .reshape(32, P, FF),
            "k2": np.asarray(inputs["L0_ffn_k2"]).view(np.uint8)
                .reshape(32, P, 8, P).transpose(2, 1, 0, 3)
                .reshape(8, P, FF),
            "s0": np.asarray(inputs["L0_ffn_s0"], np.float32).reshape(1, 1),
            "s1": np.asarray(inputs["L0_ffn_s1"], np.float32).reshape(1, 1),
            "s2": np.asarray(inputs["L0_ffn_s2"], np.float32).reshape(1, 1),
        }
        in_maps.append({k: np.ascontiguousarray(v) for k, v in m.items()})
    return in_maps


def kernel(**inputs):
    nc = build_nc()
    in_maps = _shard_inputs(inputs)
    res = run_bass_kernel_spmd(nc, in_maps, core_ids=list(range(8)))
    M = NMAIN * T
    out = np.empty((2, 16384, 1024), np.float32)
    for c in range(8):
        seq, ch = c // 4, c % 4
        out[seq, ch * M:(ch + 1) * M] = res.results[c]["y"].T
    return out


# version 41 (final)
# speedup vs baseline: 1.0694x; 1.0694x over previous
import numpy as np
from contextlib import ExitStack

import concourse.bass as bass
import concourse.mybir as mybir
from concourse.bass_utils import run_bass_kernel_spmd

F32 = mybir.dt.float32
BF16 = mybir.dt.bfloat16
U8 = mybir.dt.uint8
AF = mybir.ActivationFunctionType
ALU = mybir.AluOpType

P = 128
T = 512
G = 8
FG = 32
D = 1024
FF = 4096
EPS = 1e-5
LN05 = float(np.log(0.5))
NWARM = 2
NMAIN = 8


class Ops:
    def __init__(self, nc):
        self.nc = nc
        self.lists = {k: [] for k in ("tensor", "vector", "scalar", "sync", "gpsimd")}
        self.cnt = {}
        self.hdl = {}
        self.waited = {k: {} for k in self.lists}

    def sem(self, stack, name):
        self.hdl[name] = stack.enter_context(self.nc.semaphore(name))
        self.cnt[name] = 0

    DMA_SEMS = ("s_x", "s_w0", "s_w1", "s_w2", "s_g", "s_g1", "s_g2", "s_y")

    phase = "init"

    def emit(self, eng, fn, waits=(), inc=None, inc_amt=1):
        if inc in self.DMA_SEMS and self.cnt[inc] > 0:
            waits = list(waits) + [(inc, self.cnt[inc])]
        w = []
        for s, v in waits:
            if v is None or v <= 0:
                continue
            if self.waited[eng].get(s, -1) >= v:
                continue
            self.waited[eng][s] = v
            w.append((self.hdl[s], v))
        entry = [w, fn, None, self.phase]
        if inc is not None:
            self.cnt[inc] += inc_amt
            entry[2] = (self.hdl[inc], self.cnt[inc], inc_amt)
        self.lists[eng].append(entry)
        return self.cnt[inc] if inc is not None else None

    def val(self, s):
        return self.cnt[s]


def build_nc(n_warm=NWARM, n_main=NMAIN, taps=()):
    nc = bass.Bass("TRN2", target_bir_lowering=False,
                   detect_race_conditions=False)
    NTILE = n_warm + n_main
    NTOK = NTILE * T
    taps = set(taps)

    xin_d = nc.dram_tensor("xin", [D, NTOK], F32, kind="ExternalInput").ap()
    gate_d = nc.dram_tensor("gate", [1, 1], F32, kind="ExternalInput").ap()
    vec_names = ["ln_att_g", "ln_att_b", "ln_ffn_g", "ln_ffn_b", "att_td", "att_tf",
                 "amix_k", "amix_v", "amix_r", "fmix_k", "fmix_r"]
    vecs_d = {n: nc.dram_tensor(n, [D], F32, kind="ExternalInput").ap() for n in vec_names}
    wsq_names = ["wk", "wv", "wr", "wo", "fwr"]
    wsq_d = {n: nc.dram_tensor(n, [D, D], F32, kind="ExternalInput").ap() for n in wsq_names}
    fwk_d = nc.dram_tensor("fwk", [D, FF], F32, kind="ExternalInput").ap()
    fwv_d = nc.dram_tensor("fwv", [FF, D], F32, kind="ExternalInput").ap()
    k0_d = nc.dram_tensor("k0", [2 * D, FF], U8, kind="ExternalInput").ap()
    k1_d = nc.dram_tensor("k1", [32, P, FF], U8, kind="ExternalInput").ap()
    k2_d = nc.dram_tensor("k2", [8, P, FF], U8, kind="ExternalInput").ap()
    s_d = {n: nc.dram_tensor(n, [1, 1], F32, kind="ExternalInput").ap()
           for n in ("s0", "s1", "s2")}
    y_d = nc.dram_tensor("y", [D, n_main * T], F32, kind="ExternalOutput").ap()

    wsq_s = {n: nc.dram_tensor(n + "S", [D, D], BF16, kind="Internal").ap()
             for n in wsq_names}
    fwk_s = nc.dram_tensor("fwkS", [D, FF], BF16, kind="Internal").ap()
    fwv_s = nc.dram_tensor("fwvS", [8, FF, P], BF16, kind="Internal").ap()
    w0_s = nc.dram_tensor("w0S", [D, FF], BF16, kind="Internal").ap()

    tap_d = {}
    for tn in taps:
        shp = {"h": G * (T + 1), "k": G * T, "v": G * T, "r": G * (T + 1),
               "wkv": G * T, "x1": G * T, "h2": G * (T + 1), "kf": FG * T,
               "x2": G * T, "x4": G * T, "h1s": FG * T, "h2s": FG * T,
               "pre": G * T, "rden": G * T, "scrP": G * (T + 1),
               "st8d": 9 * 8}[tn]
        tap_d[tn] = nc.dram_tensor("tap_" + tn, [P, shp], F32,
                                   kind="ExternalOutput").ap()

    with ExitStack() as stack:
        e = stack.enter_context
        sb = lambda name, cols, dt=F32: e(nc.sbuf_tensor(name, [P, cols], dt))
        wring = [sb(f"wring{i}", 4096, BF16) for i in range(3)]
        x0 = sb("x0", G * T)
        hbuf = sb("hbuf", G * (T + 1))
        scr = sb("scr", G * (T + 1))
        xbf = sb("xbf", G * T, BF16)
        xxa = sb("xxa", G * T, BF16)
        xxb = sb("xxb", G * T, BF16)
        kvs = sb("kvs", G * T)
        vvs = sb("vvs", G * T)
        big1 = sb("big1", FG * T, BF16)
        big2 = sb("big2", FG * T, BF16)
        stg = sb("stg", T)
        ewb = sb("ewb", T)
        ewb2 = sb("ewb2", T)
        dp1 = sb("dp1", T)
        dq1 = stg
        dqb = [ewb, dq1]
        dpb = [ewb2, dp1]
        rows = e(nc.sbuf_tensor("rows", [1, 2 * T], F32))
        onesb = sb("onesb", 1, BF16)
        ones128b = sb("ones128b", P, BF16)
        ones1 = e(nc.sbuf_tensor("ones1", [1, P], F32))
        negones1 = e(nc.sbuf_tensor("negones1", [1, P], F32))
        pv = sb("pv", 12 * 8)
        st8 = sb("st8", 9 * 8)
        stash_h = sb("stash_h", 8)
        stash_h2 = sb("stash_h2", 8)
        gate_b = sb("gate_b", 1)
        sc_b = sb("sc_b", 8)
        cvals = sb("cvals", 2)

        PV = {n: i * 8 for i, n in enumerate(vec_names)}
        PV["ew"] = 11 * 8
        SP, SQ, SO, SA, SNA, SB_, SEU, SQT, SPT = [i * 8 for i in range(9)]
        ROW_MU, ROW_A = 0, T

        pbank = [e(nc.psum_tensor(f"pb{i}", [P, T], F32)) for i in range(2)]
        pstat = [e(nc.psum_tensor(f"pst{i}", [1, T], F32)) for i in range(2)]
        pbc = [e(nc.psum_tensor(f"pbc{i}", [P, T], F32)) for i in range(2)]
        ptr = [e(nc.psum_tensor(f"ptr{i}", [P, 4 * P], F32)) for i in range(2)]

        ops = Ops(nc)
        for s in ("s_x", "s_w0", "s_w1", "s_w2", "s_t", "s_v", "s_a", "s_g", "s_g1", "s_g2", "s_gc", "s_y"):
            ops.sem(stack, s)
        EMIT = ops.emit
        TE, VE, SE, SY, GP = "tensor", "vector", "scalar", "sync", "gpsimd"

        hview = hbuf[:].rearrange("p (g t) -> p g t", g=G, t=T + 1)
        sview = scr[:].rearrange("p (g t) -> p g t", g=G, t=T + 1)

        def vgrp(buf, g, c0=0, n=T, ncols=T):
            return buf[:, g * ncols + c0: g * ncols + c0 + n]

        def scol(buf, c):
            return buf[:, c:c + 1]

        bank_free = {}
        for b in pbank + pstat + pbc + ptr:
            bank_free[id(b)] = ("s_v", 0)

        def mm_group(bankap, items, wait_extra=(), bslice=None):
            s, v = bank_free[id(bankap)]
            n = len(items)
            out = bankap[:] if bslice is None else bslice
            tv = None
            for i, (lhs, rhs) in enumerate(items):
                w = (list(wait_extra) + [(s, v)]) if i == 0 else []
                tv = EMIT(TE, (lambda lhs=lhs, rhs=rhs, i=i, n=n, out=out:
                               nc.tensor.matmul(out, lhs, rhs,
                                                start=(i == 0), stop=(i == n - 1))),
                          waits=w, inc=("s_t" if i == n - 1 else None))
            return tv

        def consumed(bankap, sem, val):
            bank_free[id(bankap)] = (sem, val)

        for f in (lambda: nc.vector.memset(onesb[:], 1.0 / D),
                  lambda: nc.vector.memset(ones128b[:], 1.0),
                  lambda: nc.vector.memset(ones1[:], 1.0),
                  lambda: nc.vector.memset(negones1[:], -1.0),
                  lambda: nc.vector.memset(stash_h[:], 0.0),
                  lambda: nc.vector.memset(stash_h2[:], 0.0),
                  lambda: nc.vector.memset(st8[:, SP:SP + 8], 0.0),
                  lambda: nc.vector.memset(st8[:, SQ:SQ + 8], 0.0),
                  lambda: nc.vector.memset(st8[:, SO:SO + 8], 0.0),
                  lambda: nc.vector.memset(ewb[:], 0.0),
                  lambda: nc.vector.memset(ewb2[:], 0.0),
                  lambda: nc.vector.memset(dq1[:], 0.0),
                  lambda: nc.vector.memset(dp1[:], 0.0),
                  lambda: nc.vector.memset(cvals[:, 0:1], EPS),
                  lambda: nc.vector.memset(cvals[:, 1:2], LN05)):
            EMIT(VE, f)

        for n in vec_names:
            EMIT(SE, (lambda n=n: nc.scalar.dma_start(
                out=pv[:, PV[n]:PV[n] + 8],
                in_=vecs_d[n].rearrange("(g p) -> p g", p=P),
                allow_slow_non_contiguous=True)), inc="s_x", inc_amt=16)
        EMIT(SE, lambda: nc.scalar.dma_start(
            out=gate_b[:], in_=gate_d.broadcast_to([P, 1])), inc="s_x", inc_amt=16)
        for i, n in enumerate(("s0", "s1", "s2")):
            EMIT(SE, (lambda i=i, n=n: nc.scalar.dma_start(
                out=sc_b[:, i:i + 1], in_=s_d[n].broadcast_to([P, 1]))),
                inc="s_x", inc_amt=16)
        x_pv = ops.val("s_x")
        EMIT(SE, lambda: nc.scalar.activation(
            st8[:, SB_:SB_ + 8], pv[:, PV["att_td"]:PV["att_td"] + 8], AF.Exp),
            waits=[("s_x", x_pv)])
        EMIT(SE, lambda: nc.scalar.activation(
            st8[:, SEU:SEU + 8], pv[:, PV["att_tf"]:PV["att_tf"] + 8], AF.Exp),
            inc="s_a")
        a_params = ops.val("s_a")
        v_w = EMIT(VE, lambda: nc.vector.tensor_scalar(
            pv[:, PV["ew"]:PV["ew"] + 8], st8[:, SB_:SB_ + 8], -1.0, None,
            op0=ALU.mult), waits=[("s_a", a_params)], inc="s_v")
        EMIT(VE, lambda: nc.vector.tensor_scalar(
            sc_b[:, 0:1], sc_b[:, 0:1], 4.0, None, op0=ALU.mult),
            waits=[("s_x", x_pv)])
        EMIT(VE, lambda: nc.vector.tensor_scalar(
            sc_b[:, 3:4], sc_b[:, 1:2], 4.0, None, op0=ALU.mult))
        EMIT(VE, lambda: nc.vector.tensor_scalar(
            sc_b[:, 4:5], sc_b[:, 2:3], 1.0, None, op0=ALU.mult))
        EMIT(VE, lambda: nc.vector.tensor_scalar(
            sc_b[:, 1:2], sc_b[:, 1:2], 2.0, None, op0=ALU.mult))
        EMIT(VE, lambda: nc.vector.tensor_scalar(
            sc_b[:, 2:3], sc_b[:, 2:3], 2.0, None, op0=ALU.mult))
        EMIT(VE, lambda: nc.vector.tensor_scalar(
            sc_b[:, 5:6], sc_b[:, 1:2], -1.0, None, op0=ALU.mult))
        EMIT(VE, lambda: nc.vector.tensor_scalar(
            sc_b[:, 6:7], sc_b[:, 4:5], -1.0, None, op0=ALU.mult), inc="s_v")

        conv = {}
        for n in wsq_names:
            conv[n] = ("s_g", EMIT(GP, (lambda n=n: nc.gpsimd.dma_start(
                out=wsq_s[n][:, :], in_=wsq_d[n][:, :])), inc="s_g", inc_amt=16))
        conv["fwk"] = ("s_g", EMIT(GP, lambda: nc.gpsimd.dma_start(
            out=fwk_s[:, :], in_=fwk_d[:, :]), inc="s_g", inc_amt=16))
        gv = None
        for oc in range(8):
            gv = EMIT(GP, (lambda oc=oc: nc.gpsimd.dma_start(
                out=fwv_s[oc, :, :], in_=fwv_d[:, oc * P:(oc + 1) * P])),
                inc="s_g", inc_amt=16)
        conv["fwv"] = ("s_g", gv)

        def emit_sign_conversions():
            slot_busy = [0, 0]
            for kc in range(G):
                s0_ = (kc % 2) * 2 * FF
                gva = EMIT(GP, (lambda kc=kc, s0_=s0_: nc.gpsimd.dma_start(
                    out=big1[:, s0_:s0_ + FF], in_=k0_d[kc * P:(kc + 1) * P, :])),
                    waits=[("s_g", slot_busy[kc % 2])], inc="s_g", inc_amt=16)
                gvb = EMIT(GP, (lambda kc=kc, s0_=s0_: nc.gpsimd.dma_start(
                    out=big1[:, s0_ + FF:s0_ + 2 * FF],
                    in_=k0_d[D + kc * P:D + (kc + 1) * P, :])),
                    inc="s_g", inc_amt=16)
                cv = EMIT(VE, (lambda s0_=s0_: nc.vector.tensor_sub(
                    big1[:, s0_:s0_ + FF], big1[:, s0_:s0_ + FF],
                    big1[:, s0_ + FF:s0_ + 2 * FF])),
                    waits=[("s_g", gvb)], inc="s_v")
                slot_busy[kc % 2] = EMIT(GP, (lambda kc=kc, s0_=s0_:
                                              nc.gpsimd.dma_start(
                    out=w0_s[kc * P:(kc + 1) * P, :], in_=big1[:, s0_:s0_ + FF])),
                    waits=[("s_v", cv)], inc="s_g", inc_amt=16)
            conv["w0"] = ("s_g", max(slot_busy))

        slab = {"i": 0, "free": [("s_v", 0), ("s_v", 0), ("s_v", 0)]}
        SY_SEMS = ("s_w0", "s_w1", "s_w2")
        GP_SEMS = ("s_g", "s_g1", "s_g2")

        def load_slab(src3, kc_n, conv_wait, queue=SY):
            i = slab["i"]
            buf = wring[i % 3]
            dst = buf[:].rearrange("p (kc m) -> p kc m", kc=kc_n)
            waits = [slab["free"][i % 3]]
            if conv_wait is not None:
                waits.append(conv_wait)
            if queue == SY:
                sem = SY_SEMS[i % 3]
                wv_ = EMIT(SY, (lambda src3=src3, dst=dst: nc.sync.dma_start(
                    out=dst, in_=src3)), waits=waits, inc=sem, inc_amt=16)
            else:
                sem = GP_SEMS[i % 3]
                wv_ = EMIT(GP, (lambda src3=src3, dst=dst: nc.gpsimd.dma_start(
                    out=dst, in_=src3)), waits=waits, inc=sem, inc_amt=16)
            slab["i"] += 1
            return buf, (sem, wv_), i

        def sq_slab(scrap, c0, width=T):
            return scrap[:, c0:c0 + width].rearrange("(kc p) m -> p kc m", p=P)

        def blk_slab(scr3, blk):
            return scr3[blk, :, :].rearrange("(kc p) m -> p kc m", p=P)

        def sq_slabs(scrap, n_oc):
            return [(sq_slab(scrap, h * T), min(4, n_oc - h * 4))
                    for h in range((n_oc + 3) // 4)]

        def proj(dst_evac, mov_fn, kc_n, slabs, conv_wait, wait_extra=(),
                 mm_width=T, queue=SY, banks=None):
            banks = banks or pbank
            oc = 0
            for src3, opb in slabs:
                buf, wv_, si = load_slab(src3, kc_n, conv_wait, queue=queue)
                last_tv = None
                for j in range(opb):
                    bank = banks[oc % len(banks)]
                    if opb == 1:
                        items = [(buf[:, kc * P:(kc + 1) * P], mov_fn(kc))
                                 for kc in range(kc_n)]
                    else:
                        items = [(buf[:, kc * T + j * P: kc * T + j * P + P],
                                  mov_fn(kc)) for kc in range(kc_n)]
                    we = (list(wait_extra) if oc == 0 else []) + [wv_]
                    tv = mm_group(bank, items, wait_extra=we,
                                  bslice=(None if mm_width == T
                                          else bank[:, 0:mm_width]))
                    last_tv = tv
                    sem, val = dst_evac(oc, bank, tv)
                    consumed(bank, sem, val)
                    oc += 1
                slab["free"][si % 3] = ("s_t", last_tv)

        def ln_emit(src, dst, dst_ncols, dst_c0, gcol, bcol, ncol=T, col0=0,
                    bc_banks=None, cast_buf=None, pre_wait=None, stats_done=None,
                    apply_halves=None):
            bcb = bc_banks or pbc
            xap = cast_buf if cast_buf is not None else xbf[:]

            def sl(b, nc_=T):
                return b[:].rearrange("p (g t) -> p g t", g=G)[:, :, col0:col0 + ncol]

            slx = xap.rearrange("p (g t) -> p g t", g=G)[:, :, col0:col0 + ncol]

            def xgrp(g):
                return xap[:, g * T + col0: g * T + col0 + ncol]

            def sld(b):
                if dst_ncols == T:
                    return sl(b)
                return b[:].rearrange("p (g t) -> p g t", g=G, t=T + 1)[
                    :, :, col0:col0 + ncol]
            if stats_done is None:
                cv = EMIT(VE, lambda: nc.vector.tensor_copy(slx, sl(src)),
                          waits=([pre_wait] if pre_wait else []), inc="s_v")
                tv0 = mm_group(pstat[0],
                               [(onesb[:], xgrp(g)) for g in range(G)],
                               wait_extra=[("s_v", cv)], bslice=pstat[0][:, 0:ncol])
                cv = EMIT(VE, lambda: nc.vector.tensor_mul(slx, slx, slx),
                          waits=[("s_t", tv0)], inc="s_v")
                tv1 = mm_group(pstat[1],
                               [(onesb[:], xgrp(g)) for g in range(G)],
                               wait_extra=[("s_v", cv)], bslice=pstat[1][:, 0:ncol])
            else:
                tv0, tv1 = stats_done
            rv = EMIT(VE, lambda: nc.vector.tensor_copy(
                rows[:, ROW_MU:ROW_MU + ncol], pstat[0][:, 0:ncol]),
                waits=[("s_t", tv0)], inc="s_v")
            consumed(pstat[0], "s_v", rv)
            rv = EMIT(VE, lambda: nc.vector.tensor_mul(
                rows[:, ROW_A:ROW_A + ncol], rows[:, ROW_MU:ROW_MU + ncol],
                rows[:, ROW_MU:ROW_MU + ncol]), inc="s_v")
            rv = EMIT(VE, lambda: nc.vector.tensor_sub(
                rows[:, ROW_A:ROW_A + ncol], pstat[1][:, 0:ncol],
                rows[:, ROW_A:ROW_A + ncol]), waits=[("s_t", tv1)], inc="s_v")
            consumed(pstat[1], "s_v", rv)
            av = EMIT(SE, lambda: nc.scalar.activation(
                rows[:, ROW_A:ROW_A + ncol], rows[:, ROW_A:ROW_A + ncol],
                AF.Ln, bias=cvals[0:1, 0:1]), waits=[("s_v", rv)])
            av = EMIT(SE, lambda: nc.scalar.activation(
                rows[:, ROW_A:ROW_A + ncol], rows[:, ROW_A:ROW_A + ncol],
                AF.Exp, scale=-0.5), inc="s_a")
            rv = EMIT(VE, lambda: nc.vector.tensor_mul(
                rows[:, ROW_MU:ROW_MU + ncol], rows[:, ROW_MU:ROW_MU + ncol],
                rows[:, ROW_A:ROW_A + ncol]), waits=[("s_a", av)], inc="s_v")
            tvb1 = None
            mm_group(bcb[0], [(ones1[:], rows[:, ROW_A:ROW_A + ncol])],
                     wait_extra=[("s_v", rv)], bslice=bcb[0][:, 0:ncol])
            tvb1 = mm_group(bcb[1], [(negones1[:], rows[:, ROW_MU:ROW_MU + ncol])],
                            bslice=bcb[1][:, 0:ncol])
            rstd_b = bcb[0][:, 0:ncol].unsqueeze(1).broadcast_to([P, G, ncol])
            shift_b = bcb[1][:, 0:ncol].unsqueeze(1).broadcast_to([P, G, ncol])
            if gcol is None:
                dsl = sld(dst)
                if apply_halves is not None:
                    ev = None
                    for h in (0, 1):
                        EMIT(VE, (lambda h=h: nc.vector.tensor_mul(
                            dsl[:, 4 * h:4 * h + 4, :],
                            sl(src)[:, 4 * h:4 * h + 4, :],
                            rstd_b[:, 4 * h:4 * h + 4, :])),
                            waits=([("s_t", tvb1)] if h == 0 else []), inc="s_v")
                        ev = EMIT(VE, (lambda h=h: nc.vector.tensor_add(
                            dsl[:, 4 * h:4 * h + 4, :],
                            dsl[:, 4 * h:4 * h + 4, :],
                            shift_b[:, 4 * h:4 * h + 4, :])), inc="s_v")
                        apply_halves.append(ev)
                    consumed(bcb[0], "s_v", ev)
                    consumed(bcb[1], "s_v", ev)
                    return ("s_v", ev)
                ev = EMIT(VE, lambda: nc.vector.tensor_mul(dsl, sl(src), rstd_b),
                          waits=[("s_t", tvb1)], inc="s_v")
                ev = EMIT(VE, lambda: nc.vector.tensor_add(dsl, dsl, shift_b),
                          inc="s_v")
                consumed(bcb[0], "s_v", ev)
                consumed(bcb[1], "s_v", ev)
                return ("s_v", ev)
            ssl = sview[:, :, col0:col0 + ncol]
            ev = EMIT(VE, lambda: nc.vector.tensor_mul(ssl, sl(src), rstd_b),
                      waits=[("s_t", tvb1)], inc="s_v")
            ev = EMIT(VE, lambda: nc.vector.tensor_add(ssl, ssl, shift_b), inc="s_v")
            consumed(bcb[0], "s_v", ev)
            consumed(bcb[1], "s_v", ev)
            avs = None
            for g in range(G):
                avs = EMIT(VE, (lambda g=g: nc.vector.tensor_scalar(
                    vgrp(dst, g, dst_c0 + col0, ncol, dst_ncols),
                    vgrp(scr, g, col0, ncol, T + 1),
                    pv[:, gcol + g:gcol + g + 1],
                    pv[:, bcol + g:bcol + g + 1],
                    op0=ALU.mult, op1=ALU.add)), inc="s_v")
            return ("s_v", avs)

        stats_acc = {}

        stats_pend = {}

        def stats_oc(oc, srcap, xb, xb2, wait=None):
            w = [wait] if wait else []
            cv1 = EMIT(SE, (lambda oc=oc, srcap=srcap, xb=xb: nc.scalar.copy(
                xb[:, oc * T:(oc + 1) * T], srcap)), waits=w, inc="s_a")
            cv2 = EMIT(SE, (lambda oc=oc, srcap=srcap, xb2=xb2: nc.scalar.activation(
                xb2[:, oc * T:(oc + 1) * T], srcap, AF.Square)), inc="s_a")
            stats_pend[oc] = (cv1, cv2)

        def stats_flush(xb, xb2):
            tv0 = tv1 = None
            for oc in range(G):
                cv1, cv2 = stats_pend[oc]
                s_, v_ = bank_free[id(pstat[0])]
                tv0 = EMIT(TE, (lambda oc=oc, xb=xb: nc.tensor.matmul(
                    pstat[0][:], onesb[:], xb[:, oc * T:(oc + 1) * T],
                    start=(oc == 0), stop=(oc == G - 1))),
                    waits=[("s_a", cv1)] + ([(s_, v_)] if oc == 0 else []),
                    inc=("s_t" if oc == G - 1 else None))
                s_, v_ = bank_free[id(pstat[1])]
                tv1 = EMIT(TE, (lambda oc=oc, xb2=xb2: nc.tensor.matmul(
                    pstat[1][:], onesb[:], xb2[:, oc * T:(oc + 1) * T],
                    start=(oc == 0), stop=(oc == G - 1))),
                    waits=[("s_a", cv2)] + ([(s_, v_)] if oc == 0 else []),
                    inc=("s_t" if oc == G - 1 else None))
            stats_acc["tv"] = (tv0, tv1)

        def mix(dst, mcol, col0=0, ncol=T):
            r = None
            for g in range(G):
                r = EMIT(VE, (lambda g=g, dst=dst, mcol=mcol:
                              nc.vector.scalar_tensor_tensor(
                                  vgrp(dst, g, col0, ncol),
                                  vgrp(scr, g, col0, ncol, T + 1),
                                  pv[:, mcol + g:mcol + g + 1],
                                  hview[:, g, col0:col0 + ncol],
                                  op0=ALU.mult, op1=ALU.add)), inc="s_v")
            return r

        def tap_store(tn, buf, wait):
            if tn not in taps:
                return
            yv = EMIT(SE, (lambda tn=tn, buf=buf: nc.scalar.dma_start(
                out=tap_d[tn], in_=buf[:])), waits=[wait], inc="s_y", inc_amt=16)
            EMIT(VE, lambda: nc.vector.engine_nop(), waits=[("s_y", yv)], inc="s_v")

        x0_free = [("s_v", 0)]
        x_ld = [0]
        y_store = [0]
        m_halves = []
        kmax = [0]
        y_done = [None] * G
        v_done = [0] * G
        decay_free = {(s, i): ("s_v", 0) for s in ("q", "p") for i in (0, 1)}

        def phase_A(t_idx):
            ops.phase = "A_in"
            tok0 = t_idx * T
            lv = EMIT(SE, (lambda tok0=tok0: nc.scalar.dma_start(
                out=x0[:].rearrange("p (g t) -> p g t", g=G),
                in_=xin_d.rearrange("(g p) n -> p g n", p=P)[:, :, tok0:tok0 + T])),
                waits=[x0_free[0]], inc="s_x", inc_amt=16)
            x_ld[0] = lv

        mix_done = {}
        ijk_state = {}

        def phase_B1(t_idx, with_r, bc_banks=None):
            ops.phase = "B_ln1"
            EMIT(VE, lambda: nc.vector.tensor_copy(
                hview[:, :, 0:1], stash_h[:].rearrange("p g -> p g ()")))
            sem_h = ln_emit(x0, hbuf, T + 1, 1, PV["ln_att_g"], PV["ln_att_b"],
                            bc_banks=bc_banks, pre_wait=("s_x", x_ld[0]))
            sv = EMIT(VE, lambda: nc.vector.tensor_copy(
                stash_h[:].rearrange("p g -> p g ()"), hview[:, :, T:T + 1]),
                waits=[sem_h], inc="s_v")
            tap_store("h", hbuf, ("s_v", sv))
            EMIT(VE, lambda: nc.vector.tensor_sub(
                sview[:, :, 0:T], hview[:, :, 1:T + 1], hview[:, :, 0:T]),
                waits=[sem_h], inc="s_v")
            x0_free[0] = ("s_v", ops.val("s_v"))
            mix_done["v"] = mix(xxb, PV["amix_v"])
            mix_done["k"] = mix(xxa, PV["amix_k"])
            mix_done["r"] = mix(xbf, PV["amix_r"]) if with_r else 0

        def phase_C(t_idx, with_r, full):
            ops.phase = "C_kvr"
            EMIT(VE, lambda: nc.vector.engine_nop(),
                 waits=[("s_y", y_store[0])], inc="s_v")
            xxv_v = mix_done["v"]
            xxk_v = mix_done["k"]
            xxr_v = mix_done["r"]

            sq_done = [0] * G
            qfill = [0] * G
            pfill = [0] * G
            kva = [0] * G

            def wkv_q(g):
                c0 = g * (T + 1)
                EMIT(VE, (lambda g=g: nc.vector.tensor_max(
                    st8[:, SA + g:SA + g + 1], st8[:, SA + g:SA + g + 1],
                    st8[:, SO + g:SO + g + 1])))
                EMIT(VE, (lambda g=g: nc.vector.tensor_sub(
                    st8[:, SB_ + g:SB_ + g + 1], st8[:, SO + g:SO + g + 1],
                    st8[:, SA + g:SA + g + 1])))
                EMIT(VE, (lambda g=g: nc.vector.tensor_copy(
                    st8[:, SO + g:SO + g + 1], st8[:, SA + g:SA + g + 1])))
                av = EMIT(VE, (lambda g=g: nc.vector.tensor_scalar(
                    st8[:, SNA + g:SNA + g + 1], st8[:, SA + g:SA + g + 1],
                    -1.0, None, op0=ALU.mult)), inc="s_v")
                bv = EMIT(SE, (lambda g=g: nc.scalar.activation(
                    st8[:, SB_ + g:SB_ + g + 1], st8[:, SB_ + g:SB_ + g + 1],
                    AF.Exp)), waits=[("s_v", av), ("s_a", a_params)], inc="s_a")
                kva[g] = EMIT(SE, (lambda g=g: nc.scalar.activation(
                    vgrp(kvs, g), vgrp(kvs, g), AF.Exp,
                    bias=st8[:, SNA + g:SNA + g + 1])), inc="s_a")
                qtv = EMIT(VE, (lambda g=g: nc.vector.tensor_mul(
                    st8[:, SQT + g:SQT + g + 1], st8[:, SQ + g:SQ + g + 1],
                    st8[:, SB_ + g:SB_ + g + 1])),
                    waits=[("s_a", kva[g])], inc="s_v")
                EMIT(VE, (lambda g=g, c0=c0: nc.vector.tensor_copy(
                    scr[:, c0:c0 + 1], st8[:, SQT + g:SQT + g + 1])),
                    waits=[("s_v", qtv)])
                sqv = EMIT(VE, (lambda g=g, c0=c0: nc.vector.tensor_tensor_scan(
                    vgrp(scr, g, 1, T, T + 1), dqb[g % 2][:], vgrp(kvs, g),
                    st8[:, SQT + g:SQT + g + 1], op0=ALU.mult, op1=ALU.add)),
                    waits=[("s_a", qfill[g])], inc="s_v")
                EMIT(VE, (lambda g=g, c0=c0: nc.vector.tensor_copy(
                    st8[:, SQ + g:SQ + g + 1], scr[:, c0 + T:c0 + T + 1])),
                    waits=[("s_v", sqv)], inc="s_v")
                decay_free[("q", g % 2)] = ("s_v", sqv)
                sq_done[g] = sqv

            def wkv_p(g):
                c0 = g * (T + 1)
                EMIT(VE, (lambda g=g: nc.vector.tensor_mul(
                    vgrp(vvs, g), vgrp(kvs, g), vgrp(vvs, g))),
                    waits=[("s_a", v_done[g]), ("s_a", kva[g])], inc="s_v")
                ln_a = None
                if full:
                    dn = EMIT(VE, (lambda g=g: nc.vector.scalar_tensor_tensor(
                        vgrp(kvs, g), vgrp(kvs, g),
                        st8[:, SEU + g:SEU + g + 1],
                        vgrp(scr, g, 0, T, T + 1), op0=ALU.mult, op1=ALU.add)),
                        waits=[("s_v", sq_done[g])], inc="s_v")
                    EMIT(SE, (lambda g=g: nc.scalar.activation(
                        vgrp(kvs, g), vgrp(kvs, g), AF.Ln)),
                        waits=[("s_v", dn)])
                    ln_a = EMIT(SE, (lambda g=g: nc.scalar.activation(
                        vgrp(kvs, g), vgrp(kvs, g), AF.Exp, scale=-1.0,
                        bias=cvals[:, 1:2])), inc="s_a")
                ptv = EMIT(VE, (lambda g=g: nc.vector.tensor_mul(
                    st8[:, SPT + g:SPT + g + 1], st8[:, SP + g:SP + g + 1],
                    st8[:, SB_ + g:SB_ + g + 1])), inc="s_v")
                EMIT(VE, (lambda g=g, c0=c0: nc.vector.tensor_copy(
                    scr[:, c0:c0 + 1], st8[:, SPT + g:SPT + g + 1])),
                    waits=[("s_v", ptv)])
                spv = EMIT(VE, (lambda g=g, c0=c0: nc.vector.tensor_tensor_scan(
                    vgrp(scr, g, 1, T, T + 1), dpb[g % 2][:], vgrp(vvs, g),
                    st8[:, SPT + g:SPT + g + 1], op0=ALU.mult, op1=ALU.add)),
                    waits=[("s_a", pfill[g])], inc="s_v")
                EMIT(VE, (lambda g=g, c0=c0: nc.vector.tensor_copy(
                    st8[:, SP + g:SP + g + 1], scr[:, c0 + T:c0 + T + 1])),
                    waits=[("s_v", spv)], inc="s_v")
                decay_free[("p", g % 2)] = ("s_v", spv)
                if full:
                    EMIT(VE, (lambda g=g: nc.vector.scalar_tensor_tensor(
                        vgrp(vvs, g), vgrp(vvs, g), st8[:, SEU + g:SEU + g + 1],
                        vgrp(scr, g, 0, T, T + 1), op0=ALU.mult, op1=ALU.add)))
                    yv = EMIT(VE, (lambda g=g: nc.vector.tensor_mul(
                        vgrp(vvs, g), vgrp(vvs, g), vgrp(kvs, g))),
                        waits=([("s_a", ln_a)] if full else []), inc="s_v")
                    y_done[g] = ("s_v", yv)
                else:
                    y_done[g] = ("s_v", ops.val("s_v"))

            def k_evac(oc, bank, tv):
                ev = EMIT(SE, (lambda oc=oc, bank=bank: nc.scalar.copy(
                    vgrp(kvs, oc), bank[:])),
                    waits=[("s_t", tv)] + ([y_done[oc]] if y_done[oc] else []),
                    inc="s_a")
                qfill[oc] = EMIT(SE, (lambda oc=oc: nc.scalar.activation(
                    dqb[oc % 2][:], dqb[oc % 2][:], AF.Exp, scale=0.0,
                    bias=pv[:, PV["ew"] + oc:PV["ew"] + oc + 1])),
                    waits=[decay_free[("q", oc % 2)], ("s_v", v_w)], inc="s_a")
                EMIT(VE, (lambda oc=oc: nc.vector.reduce_max(
                    st8[:, SA + oc:SA + oc + 1], vgrp(kvs, oc),
                    axis=mybir.AxisListType.X)), waits=[("s_a", ev)], inc="s_v")
                if oc >= 1:
                    wkv_q(oc - 1)
                return "s_a", ev

            proj(k_evac, lambda kc: xxa[:, kc * T:(kc + 1) * T], G,
                 sq_slabs(wsq_s["wk"], G), conv["wk"], wait_extra=[("s_v", xxk_v)],
                 banks=pbank + pbc)
            wkv_q(G - 1)

            def v_evac(oc, bank, tv):
                ev = EMIT(SE, (lambda oc=oc, bank=bank: nc.scalar.copy(
                    vgrp(vvs, oc), bank[:])),
                    waits=[("s_t", tv)] + ([y_done[oc]] if y_done[oc] else []),
                    inc="s_a")
                pfill[oc] = EMIT(SE, (lambda oc=oc: nc.scalar.activation(
                    dpb[oc % 2][:], dpb[oc % 2][:], AF.Exp, scale=0.0,
                    bias=pv[:, PV["ew"] + oc:PV["ew"] + oc + 1])),
                    waits=[decay_free[("p", oc % 2)], ("s_v", v_w)], inc="s_a")
                v_done[oc] = ev
                wkv_p(oc)
                return "s_a", ev

            proj(v_evac, lambda kc: xxb[:, kc * T:(kc + 1) * T], G,
                 sq_slabs(wsq_s["wv"], G), conv["wv"], wait_extra=[("s_v", xxv_v)],
                 banks=pbank + pbc)

            if with_r:
                def r_evac(oc, bank, tv):
                    av = EMIT(SE, (lambda oc=oc, bank=bank: nc.scalar.activation(
                        vgrp(hbuf, oc, 1, T, T + 1), bank[:], AF.Tanh, scale=0.5)),
                        waits=[("s_t", tv)], inc="s_a")
                    EMIT(VE, (lambda oc=oc: nc.vector.scalar_tensor_tensor(
                        vgrp(xxa, oc), hview[:, oc, 1:T + 1], 1.0,
                        vgrp(vvs, oc), op0=ALU.add, op1=ALU.mult)),
                        waits=[("s_a", av), y_done[oc]], inc="s_v")
                    return "s_a", av

                proj(r_evac, lambda kc: xbf[:, kc * T:(kc + 1) * T], G,
                     sq_slabs(wsq_s["wr"], G), conv["wr"],
                     wait_extra=[("s_v", xxr_v)], banks=pbank + pbc)
            return ops.val("s_v")

        def rw_emit(col0=0, ncol=T, wait_extra=()):
            r = None
            for g in range(G):
                r = EMIT(VE, (lambda g=g: nc.vector.scalar_tensor_tensor(
                    vgrp(xxa, g, col0, ncol),
                    hview[:, g, 1 + col0:1 + col0 + ncol],
                    1.0, vgrp(vvs, g, col0, ncol), op0=ALU.add, op1=ALU.mult)),
                    waits=(list(wait_extra) if g == 0 else []), inc="s_v")
            return r

        def phase_E(rw_v):
            ops.phase = "E_wo"

            def wo_evac(oc, bank, tv):
                ev = EMIT(VE, (lambda oc=oc, bank=bank: nc.vector.tensor_add(
                    vgrp(vvs, oc), vgrp(x0, oc), bank[:])),
                    waits=[("s_t", tv)], inc="s_v")
                stats_oc(oc, vgrp(vvs, oc), xbf, xxb, wait=("s_v", ev))
                return "s_v", ev

            proj(wo_evac, lambda kc: xxa[:, kc * T:(kc + 1) * T], G,
                 sq_slabs(wsq_s["wo"], G), conv["wo"], wait_extra=[("s_v", rw_v)],
                 banks=pbank + pbc)
            stats_flush(xbf, xxb)

        def phase_FG():
            ops.phase = "F_ln2"
            EMIT(VE, lambda: nc.vector.tensor_copy(
                hview[:, :, 0:1], stash_h2[:].rearrange("p g -> p g ()")))
            sem_h2 = ln_emit(vvs, hbuf, T + 1, 1, PV["ln_ffn_g"], PV["ln_ffn_b"],
                             stats_done=stats_acc["tv"])
            sv = EMIT(VE, lambda: nc.vector.tensor_copy(
                stash_h2[:].rearrange("p g -> p g ()"), hview[:, :, T:T + 1]),
                waits=[sem_h2], inc="s_v")
            tap_store("h2", hbuf, ("s_v", sv))
            EMIT(VE, lambda: nc.vector.tensor_sub(
                sview[:, :, 0:T], hview[:, :, 1:T + 1], hview[:, :, 0:T]),
                waits=[sem_h2], inc="s_v")
            ops.phase = "G_ffn"
            xxk2_v = mix(xxa, PV["fmix_k"])
            xxr2_v = mix(xxb, PV["fmix_r"])
            def kf_evac(oc, bank, tv):
                rv = EMIT(VE, (lambda bank=bank: nc.vector.tensor_scalar_max(
                    stg[:], bank[:], 0.0)), waits=[("s_t", tv)], inc="s_v")
                EMIT(VE, (lambda oc=oc: nc.vector.tensor_mul(
                    vgrp(big1, oc), stg[:], stg[:])),
                    waits=[conv["w0"]], inc="s_v")
                return "s_v", rv

            proj(kf_evac, lambda kc: xxa[:, kc * T:(kc + 1) * T], G,
                 sq_slabs(fwk_s, FG), conv["fwk"], wait_extra=[("s_v", xxk2_v)],
                 banks=pbank + pbc)

            def rf_evac(oc, bank, tv):
                av = EMIT(SE, (lambda oc=oc, bank=bank: nc.scalar.activation(
                    vgrp(hbuf, oc, 1, T, T + 1), bank[:], AF.Tanh, scale=0.5)),
                    waits=[("s_t", tv)], inc="s_a")
                return "s_a", av

            proj(rf_evac, lambda kc: xxb[:, kc * T:(kc + 1) * T], G,
                 sq_slabs(wsq_s["fwr"], G), conv["fwr"],
                 wait_extra=[("s_v", xxr2_v)], banks=pbank + pbc)
            return ops.val("s_a")

        def phase_H(rf_a):
            ops.phase = "H_fwv"
            hv0 = ops.val("s_v")

            def fwv_evac(oc, bank, tv):
                h = oc % 2
                dst = stg[:] if h == 0 else ewb2[:]
                EMIT(VE, (lambda oc=oc, bank=bank, dst=dst:
                          nc.vector.scalar_tensor_tensor(
                              dst, hview[:, oc, 1:T + 1], 1.0, bank[:],
                              op0=ALU.add, op1=ALU.mult)),
                     waits=[("s_t", tv), ("s_a", rf_a), ("s_v", hv0)])
                EMIT(VE, (lambda oc=oc, dst=dst: nc.vector.scalar_tensor_tensor(
                    vgrp(vvs, oc), dst, 0.5, vgrp(vvs, oc),
                    op0=ALU.mult, op1=ALU.add)))
                ev = EMIT(VE, (lambda oc=oc: nc.vector.tensor_add(
                    vgrp(x0, oc), vgrp(x0, oc), vgrp(vvs, oc))), inc="s_v")
                stats_oc(oc, vgrp(x0, oc), xbf, xxa, wait=("s_v", ev))
                return "s_v", ev

            proj(fwv_evac, lambda kc: big1[:, kc * T:(kc + 1) * T], FG,
                 [(blk_slab(fwv_s, oc), 1) for oc in range(8)], conv["fwv"],
                 banks=pbank + pbc)
            stats_flush(xbf, xxa)
            decay_free[("p", 0)] = ("s_v", ops.val("s_v"))
            decay_free[("q", 1)] = ("s_v", ops.val("s_v"))

        def phase_IJK(t_idx):
            ops.phase = "I_pln1"
            ln_emit(x0, kvs, T, 0, None, None, stats_done=stats_acc["tv"])
            x0_free[0] = ("s_v", ops.val("s_v"))
            cv = EMIT(VE, lambda: nc.vector.tensor_copy(xbf[:], kvs[:]), inc="s_v")
            tap_store("x4", kvs, ("s_v", cv))

            def h1_evac(oc, bank, tv):
                av = EMIT(VE, (lambda oc=oc, bank=bank: nc.vector.tensor_scalar(
                    vgrp(big1, oc), bank[:], sc_b[:, 0:1], 0.0,
                    op0=ALU.mult, op1=ALU.max)),
                    waits=[("s_t", tv)], inc="s_v")
                return "s_v", av

            ops.phase = "J_w0"
            proj(h1_evac, lambda kc: xbf[:, kc * T:(kc + 1) * T], G,
                 sq_slabs(w0_s, FG), conv["w0"], wait_extra=[("s_v", cv)],
                 banks=pbank + pbc)
            if "h1s" in taps:
                tap_via_f32("h1s", big1, FG)

            ops.phase = "K_w1"
            h1_a = ops.val("s_v")
            tvS = mm_group(pbc[0],
                           [(ones128b[:], vgrp(big1, kc)) for kc in range(FG)],
                           wait_extra=[("s_v", h1_a)])
            bcv = EMIT(VE, lambda: nc.vector.tensor_scalar(
                ewb[:], pbc[0][:], sc_b[:, 5:6], None, op0=ALU.mult),
                waits=[("s_t", tvS)], inc="s_v")
            consumed(pbc[0], "s_v", bcv)
            def h2s_evac(oc, bank, tv):
                h = oc % 2
                dst = stg[:] if h == 0 else ewb2[:]
                EMIT(VE, (lambda bank=bank, dst=dst:
                          nc.vector.scalar_tensor_tensor(
                              dst, bank[:], sc_b[:, 3:4], ewb[:],
                              op0=ALU.mult, op1=ALU.add)),
                     waits=[("s_t", tv), ("s_v", bcv)])
                ev = EMIT(VE, (lambda oc=oc, dst=dst: nc.vector.tensor_scalar_max(
                    vgrp(big2, oc), dst, 0.0)), inc="s_v")
                return "s_v", ev

            proj(h2s_evac, lambda kc: big1[:, kc * T:(kc + 1) * T], FG,
                 [(k1_d[mc].rearrange("p (kc m) -> p kc m", kc=32), 1)
                  for mc in range(32)], None, queue=GP,
                 banks=[pbank[0], pbank[1], pbc[1], pbc[0]])
            ijk_state["la"] = ("s_v", ops.val("s_v"))
            ijk_state["lastk"] = ops.val("s_v")
            if "h2s" in taps:
                tap_via_f32("h2s", big2, FG)

        def phase_LM(t_idx):
            ops.phase = "L_w2"
            tvS2 = mm_group(pbc[1],
                            [(ones128b[:], vgrp(big2, kc)) for kc in range(FG)],
                            wait_extra=[ijk_state["la"]])
            bcv2 = EMIT(VE, lambda: nc.vector.tensor_scalar(
                ewb[:], pbc[1][:], sc_b[:, 6:7], None, op0=ALU.mult),
                waits=[("s_t", tvS2), ("s_v", ijk_state["lastk"])], inc="s_v")
            consumed(pbc[1], "s_v", bcv2)

            def x5_evac(oc, bank, tv):
                EMIT(VE, (lambda oc=oc, bank=bank:
                          nc.vector.scalar_tensor_tensor(
                              vgrp(vvs, oc), bank[:], sc_b[:, 2:3],
                              vgrp(kvs, oc), op0=ALU.mult, op1=ALU.add)),
                     waits=[("s_t", tv), ("s_v", bcv2)])
                ev = EMIT(VE, (lambda oc=oc: nc.vector.tensor_add(
                    vgrp(vvs, oc), vgrp(vvs, oc), ewb[:])), inc="s_v")
                stats_oc(oc, vgrp(vvs, oc), big1,
                         big1[:, G * T:2 * G * T].rearrange('p c -> p c'),
                         wait=("s_v", ev))
                return "s_v", ev

            proj(x5_evac, lambda kc: big2[:, kc * T:(kc + 1) * T], FG,
                 [(k2_d[oc].rearrange("p (kc m) -> p kc m", kc=32), 1)
                  for oc in range(8)], None, queue=GP,
                 banks=[pbank[0], pbank[1], pbc[0], pbc[1]])
            stats_flush(big1, big1[:, G * T:2 * G * T])
            decay_free[("q", 0)] = ("s_v", ops.val("s_v"))
            ops.phase = "M_pln2"
            m_halves.clear()
            ln_emit(vvs, scr, T + 1, 0, None, None, cast_buf=big1[:, 0:G * T],
                    stats_done=stats_acc["tv"], apply_halves=m_halves)

        def tap_via_f32(tn, bfbuf, ng):
            for part in range(ng // G):
                cv = EMIT(VE, (lambda part=part: nc.vector.tensor_copy(
                    scr[:, 0:G * T].rearrange("p (g t) -> p g t", g=G),
                    bfbuf[:, part * G * T:(part + 1) * G * T].rearrange(
                        "p (g t) -> p g t", g=G))), inc="s_v")
                yv = EMIT(SE, (lambda tn=tn, part=part: nc.scalar.dma_start(
                    out=tap_d[tn][:, part * G * T:(part + 1) * G * T],
                    in_=scr[:, 0:G * T])), waits=[("s_v", cv)],
                    inc="s_y", inc_amt=16)
                EMIT(VE, lambda: nc.vector.engine_nop(), waits=[("s_y", yv)],
                     inc="s_v")

        def phase_N(t_idx):
            ops.phase = "N_out"
            t0c = (t_idx - n_warm) * T
            for g in range(G):
                mv = m_halves[g // 4]
                yv = EMIT(SE, (lambda g=g, t0c=t0c: nc.scalar.dma_start(
                    out=y_d[g * P:(g + 1) * P, t0c:t0c + T],
                    in_=vgrp(scr, g, 0, T, T + 1))),
                    waits=[("s_v", mv)], inc="s_y", inc_amt=16)
            y_store[0] = ops.val("s_y")

        def bridge():
            ops.phase = "X_bridge"
            EMIT(VE, lambda: nc.vector.engine_nop(),
                 waits=[("s_gc", ops.val("s_gc"))], inc="s_v")
            c0 = T - P
            EMIT(VE, lambda: nc.vector.tensor_sub(
                sview[:, :, c0:T], hview[:, :, c0 + 1:T + 1], hview[:, :, c0:T]),
                inc="s_v")
            xxr_v = mix(xbf, PV["amix_r"], col0=c0, ncol=P)

            def r_evac(oc, bank, tv):
                av = EMIT(SE, (lambda oc=oc, bank=bank: nc.scalar.activation(
                    vgrp(hbuf, oc, 1 + c0, P, T + 1), bank[:, 0:P],
                    AF.Tanh, scale=0.5)), waits=[("s_t", tv)], inc="s_a")
                return "s_a", av

            proj(r_evac, lambda kc: xbf[:, kc * T + c0: kc * T + c0 + P], G,
                 sq_slabs(wsq_s["wr"], G), conv["wr"],
                 wait_extra=[("s_v", xxr_v)], mm_width=P)
            rwv = rw_emit(col0=c0, ncol=P, wait_extra=[("s_a", ops.val("s_a"))])

            def wo_evac(oc, bank, tv):
                ev = EMIT(VE, (lambda oc=oc, bank=bank: nc.vector.tensor_add(
                    vgrp(vvs, oc, c0, P), vgrp(x0, oc, c0, P), bank[:, 0:P])),
                    waits=[("s_t", tv)], inc="s_v")
                return "s_v", ev

            proj(wo_evac, lambda kc: xxa[:, kc * T + c0: kc * T + c0 + P], G,
                 sq_slabs(wsq_s["wo"], G), conv["wo"],
                 wait_extra=[("s_v", rwv)], mm_width=P)
            ln_emit(vvs, hbuf, T + 1, 1, PV["ln_ffn_g"], PV["ln_ffn_b"],
                    ncol=P, col0=c0)
            EMIT(VE, lambda: nc.vector.tensor_copy(
                stash_h2[:].rearrange("p g -> p g ()"), hview[:, :, T:T + 1]),
                waits=[("s_a", ops.val("s_a"))], inc="s_v")
            gb8 = gate_b[:, 0:1].broadcast_to([P, 8])
            for off in (SP, SQ, SO):
                EMIT(VE, (lambda off=off: nc.vector.tensor_mul(
                    st8[:, off:off + 8], st8[:, off:off + 8], gb8)))
            EMIT(VE, lambda: nc.vector.tensor_mul(stash_h[:], stash_h[:], gb8))
            EMIT(VE, lambda: nc.vector.tensor_mul(stash_h2[:], stash_h2[:], gb8),
                 inc="s_v")

        if n_warm == 0:
            emit_sign_conversions()
        hoisted = [False]
        for t_idx in range(NTILE):
            is_main = t_idx >= n_warm
            if not hoisted[0]:
                phase_A(t_idx)
                phase_B1(t_idx, with_r=is_main)
            hoisted[0] = False
            full = is_main or t_idx == n_warm - 1
            rwv = phase_C(t_idx, with_r=is_main, full=full)
            if t_idx == 0 and not is_main:
                ops.phase = "S_conv"
                emit_sign_conversions()
            if is_main:
                tap_store("wkv", vvs, ("s_v", ops.val("s_v")))
                phase_E(rwv)
                tap_store("x1", vvs, ("s_v", ops.val("s_v")))
                rf_a = phase_FG()
                phase_H(rf_a)
                tap_store("x2", vvs, ("s_v", ops.val("s_v")))
                phase_IJK(t_idx)
                if t_idx + 1 < NTILE:
                    phase_A(t_idx + 1)
                    phase_B1(t_idx + 1, with_r=True, bc_banks=ptr)
                    hoisted[0] = True
                phase_LM(t_idx)
                phase_N(t_idx)
            elif t_idx == n_warm - 1:
                bridge()

        EMIT(SY, lambda: None, waits=[("s_y", ops.val("s_y"))])

        def make_runner(lst):
            def run(eng_obj):
                cur = None
                sid = None
                for w, fn, inc, phase in lst:
                    for h, v in w:
                        eng_obj.wait_ge(h, v)
                    if fn is not None:
                        if phase != cur:
                            if cur is not None:
                                nc.leave_named_scope(cur, sid, False)
                            sid, _ = nc.enter_named_scope(phase, False)
                            cur = phase
                        instr = fn()
                        if inc is not None:
                            instr.then_inc(inc[0], inc[2])
                if cur is not None:
                    nc.leave_named_scope(cur, sid, False)
            return run

        runners = {k: make_runner(v) for k, v in ops.lists.items()}
        block = e(nc.Block())

        @block.tensor
        def _(engine):
            runners["tensor"](engine)

        @block.vector
        def _(engine):
            runners["vector"](engine)

        @block.scalar
        def _(engine):
            runners["scalar"](engine)

        @block.sync
        def _(engine):
            runners["sync"](engine)

        @block.gpsimd
        def _(engine):
            runners["gpsimd"](engine)

    return nc


def _shard_inputs(inputs, n_warm=NWARM, n_main=NMAIN):
    x = np.asarray(inputs["L0_attn_x"])
    W = n_warm * T
    M = n_main * T
    in_maps = []
    for c in range(8):
        seq, ch = c // 4, c % 4
        start = ch * M
        xs = x[seq]
        if ch == 0:
            if W > 0:
                xw = np.concatenate([xs[0:W], xs[0:M]], axis=0)
            else:
                xw = xs[0:M]
            gate = np.zeros((1, 1), np.float32)
        else:
            xw = xs[start - W:start + M]
            gate = np.ones((1, 1), np.float32)
        xw = np.ascontiguousarray(xw.T)
        m = {
            "xin": xw, "gate": gate,
            "ln_att_g": inputs["L0_attn_ln_att_g"],
            "ln_att_b": inputs["L0_attn_ln_att_b"],
            "ln_ffn_g": inputs["L0_attn_ln_ffn_g"],
            "ln_ffn_b": inputs["L0_attn_ln_ffn_b"],
            "att_td": inputs["L0_attn_att_time_decay"],
            "att_tf": inputs["L0_attn_att_time_first"],
            "amix_k": inputs["L0_attn_att_mix_k"],
            "amix_v": inputs["L0_attn_att_mix_v"],
            "amix_r": inputs["L0_attn_att_mix_r"],
            "fmix_k": inputs["L0_attn_ffn_mix_k"],
            "fmix_r": inputs["L0_attn_ffn_mix_r"],
            "wk": inputs["L0_attn_att_wk"], "wv": inputs["L0_attn_att_wv"],
            "wr": inputs["L0_attn_att_wr"], "wo": inputs["L0_attn_att_wo"],
            "fwk": inputs["L0_attn_ffn_wk"], "fwv": inputs["L0_attn_ffn_wv"],
            "fwr": inputs["L0_attn_ffn_wr"],
            "k0": np.asarray(inputs["L0_ffn_k0"]).view(np.uint8),
            "k1": np.asarray(inputs["L0_ffn_k1"]).view(np.uint8)
                .reshape(32, P, 32, P).transpose(2, 1, 0, 3)
                .reshape(32, P, FF),
            "k2": np.asarray(inputs["L0_ffn_k2"]).view(np.uint8)
                .reshape(32, P, 8, P).transpose(2, 1, 0, 3)
                .reshape(8, P, FF),
            "s0": np.asarray(inputs["L0_ffn_s0"], np.float32).reshape(1, 1),
            "s1": np.asarray(inputs["L0_ffn_s1"], np.float32).reshape(1, 1),
            "s2": np.asarray(inputs["L0_ffn_s2"], np.float32).reshape(1, 1),
        }
        in_maps.append({k: np.ascontiguousarray(v) for k, v in m.items()})
    return in_maps


def kernel(**inputs):
    nc = build_nc()
    in_maps = _shard_inputs(inputs)
    res = run_bass_kernel_spmd(nc, in_maps, core_ids=list(range(8)))
    M = NMAIN * T
    out = np.empty((2, 16384, 1024), np.float32)
    for c in range(8):
        seq, ch = c // 4, c % 4
        out[seq, ch * M:(ch + 1) * M] = res.results[c]["y"].T
    return out
